# version 07 (final)
# baseline (speedup 1.0000x reference)
import numpy as np
import ml_dtypes
from contextlib import ExitStack
import concourse.bass as bass
import concourse.mybir as mybir
from concourse.bass_utils import run_bass_kernel_spmd

F32 = mybir.dt.float32
BF16 = mybir.dt.bfloat16
I32 = mybir.dt.int32
U8 = mybir.dt.uint8
AF = mybir.ActivationFunctionType
ALU = mybir.AluOpType
AX = mybir.AxisListType
BF = ml_dtypes.bfloat16

D = 1024
DC = 8
TG = 512
EPS = 1e-5

ENGS = ("sync", "vector", "scalar", "tensor", "gpsimd")


def max_tok(*toks):
    toks = [t for t in toks if t is not None]
    if len(toks) == 1:
        return toks[0]
    return ("__multi__", toks)


class Ct:
    def __init__(self):
        self.n = 0
    def bump(self, k=1):
        self.n += k
        return self.n


class Launch:
    def __init__(self, name):
        self.name = name
        self.nc = bass.Bass("TRN2", target_bir_lowering=False, debug=False)
        self.ctx = ExitStack()
        self.prog = {e: [] for e in ENGS}
        self.in_names = []
        self.out_names = []
        self._sems = {}
        self.ct = {}
        self.uid = 0
        self.rot_i = 0
        self.NB = 4
        self.providers = {}
        self._finalized = None

    def inp(self, name, shape, dt=F32):
        self.in_names.append(name)
        return self.nc.dram_tensor(name, list(shape), dt, kind="ExternalInput").ap()

    def out(self, name, shape, dt=F32):
        self.out_names.append(name)
        return self.nc.dram_tensor(name, list(shape), dt, kind="ExternalOutput").ap()

    def dram(self, name, shape, dt=F32):
        return self.nc.dram_tensor(name, list(shape), dt, kind="Internal").ap()

    def sb(self, name, shape, dt=F32):
        return self.ctx.enter_context(self.nc.sbuf_tensor(name, list(shape), dt))

    def psum(self, name, shape, dt=F32):
        return self.ctx.enter_context(self.nc.psum_tensor(name, list(shape), dt))

    def psum_scope(self):
        from contextlib import ExitStack
        return ExitStack()

    def psum_in(self, scope, name, shape, dt=F32):
        return scope.enter_context(self.nc.psum_tensor(name, list(shape), dt))

    def sb_in(self, scope, name, shape, dt=F32):
        return scope.enter_context(self.nc.sbuf_tensor(name, list(shape), dt))

    def make_arena(self, kb):
        self.arena = self.sb("arena", [128, kb * 1024], U8)

    def carve(self, off_kb, shape, dt=F32):
        esz = {F32: 4, BF16: 2, I32: 4, U8: 1}[dt]
        n = 1
        for s in shape[1:]:
            n *= s
        off = int(off_kb * 1024)
        v = self.arena[:, off:off + n * esz].bitcast(dt)
        if len(shape) == 2:
            return v
        assert len(shape) == 3
        return v.rearrange("p (a b) -> p a b", a=shape[1])

    def sem(self, name):
        if name not in self._sems:
            self._sems[name] = self.ctx.enter_context(self.nc.semaphore(name))
            self.ct[name] = Ct()
        return self._sems[name]

    def w(self, eng, semname, val=None):
        if semname == "__multi__":
            for t in val:
                self.w(eng, *t)
            return
        s = self.sem(semname)
        if val and val > 0:
            self.prog[eng].append(lambda e: e.wait_ge(s, val))

    def cnt(self, semname):
        return self.ct[semname].n if semname in self.ct else 0

    def do(self, eng, fn, inc=None):
        if inc is None:
            self.prog[eng].append(fn)
            return None
        s = self.sem(inc[0])
        v = self.ct[inc[0]].bump(inc[1])
        self.prog[eng].append(lambda e: fn(e).then_inc(s, inc[1]))
        return v

    def dma(self, eng, out_ap, in_ap, semname=None, **kw):
        if semname is None:
            semname = f"s_d{len(self._sems)}"
        v = self.do(eng, lambda e: e.dma_start(out=out_ap, in_=in_ap, **kw),
                    inc=(semname, 16))
        return (semname, v)

    def wtok(self, eng, tok):
        self.w(eng, tok[0], tok[1])

    def rot_tile(self, matmul_fns, TGT=TG, mm_dep=None):
        i = self.rot_i
        self.rot_i += 1
        b = i % self.NB
        ps = self.psR[:, b * TG:b * TG + TGT]
        if not hasattr(self, "rot_ledger"):
            self.rot_ledger = []
        if mm_dep is not None:
            self.w("tensor", *mm_dep)
        if i >= self.NB:
            self.w("tensor", *self.rot_ledger[i - self.NB])
        n = len(matmul_fns)
        for j, fn in enumerate(matmul_fns):
            first, last = (j == 0), (j == n - 1)
            if last:
                self.do("tensor", lambda e, fn=fn, first=first: fn(e, ps, first, True),
                        inc=("s_rmm", 1))
            else:
                self.do("tensor", lambda e, fn=fn, first=first: fn(e, ps, first, False))
        return ps

    def rot_evac(self, eng, fns, evsem=None, dep=None):
        if evsem is None:
            evsem = "s_rv_" + eng
        if dep is not None:
            self.w(eng, *dep)
        self.w(eng, "s_rmm", self.ct["s_rmm"].n)
        tok = None
        for j, fn in enumerate(fns):
            if j == len(fns) - 1:
                v = self.do(eng, fn, inc=(evsem, 1))
                tok = (evsem, v)
            else:
                self.do(eng, fn)
        self.rot_ledger.append(tok)
        return tok

    def aux_guard(self):
        self.sem("s_aev")
        self.w("tensor", "s_aev", self.cnt("s_aev"))

    def aux_evac(self, eng, fns):
        self.w(eng, "s_amm", self.cnt("s_amm"))
        for j, fn in enumerate(fns):
            if j == len(fns) - 1:
                self.do(eng, fn, inc=("s_aev", 1))
            else:
                self.do(eng, fn)

    def setup_common(self, maxT=2048, onesT=False):
        C = self
        C.psR = C.psum("psR", [128, 4 * TG], F32)
        C._auxscope = C.psum_scope()
        C._auxn = 0
        C.alloc_aux()
        C.id16 = C.sb("id16", [128, 128], BF16)
        C.ones_col = C.sb("ones_col", [128, 1], F32)
        C.ones_colb = C.sb("ones_colb", [128, 1], BF16)
        C.ones128 = C.sb("ones128", [1, 128], F32)
        C.ones128b = C.sb("ones128b", [1, 128], BF16)
        C.t_ln = C.sb("t_ln", [128, TG], F32)
        C.xsq = C.sb("xsq", [128, 4, TG], F32)
        C.row = C.sb("lnrow", [1, 5, maxT], F32)
        id_in = C.inp("c_id16", [128, 128], BF16)
        C.providers["c_id16"] = lambda core: np.eye(128).astype(BF)
        tok = C.dma("sync", C.id16[:], id_in)
        C.do("gpsimd", lambda e: e.memset(C.ones_col[:], 1.0))
        C.do("gpsimd", lambda e: e.memset(C.ones_colb[:], 1.0))
        C.do("gpsimd", lambda e: e.memset(C.ones128b[:], 1.0))
        C.do("gpsimd", lambda e: e.memset(C.ones128[:], 1.0), inc=("s_pr", 1))
        if onesT:
            C.onesT = C.sb("onesT", [1, maxT], F32)
            C.do("gpsimd", lambda e: e.memset(C.onesT[:], 1.0), inc=("s_pr", 1))
        C.wtok("tensor", tok)
        C.w("tensor", "s_pr", self.cnt("s_pr"))

    def alloc_aux(self):
        C = self
        n = C._auxn
        C.psS = C.psum_in(C._auxscope, f"psS{n}", [128, TG], F32)
        C.psO = C.psum_in(C._auxscope, f"psO{n}", [128, TG], F32)
        C.psU = C.psS
        C.psT = C.psum_in(C._auxscope, f"psT{n}", [128, 2048], BF16)
        C._auxn += 1

    def free_aux(self):
        self._auxscope.close()
        self._auxscope = self.psum_scope()

    def const_in(self, name, arr_or_fn):
        fn = arr_or_fn if callable(arr_or_fn) else (lambda core, a=arr_or_fn: a)
        a0 = fn(0)
        dtmap = {np.dtype(np.float32): F32, np.dtype(np.int32): I32,
                 np.dtype(BF): BF16, np.dtype(np.uint8): U8}
        ap = self.inp(name, a0.shape, dtmap[a0.dtype])
        self.providers[name] = fn
        return ap

    def finalize(self):
        if self._finalized is None:
            if hasattr(self, "_auxscope"):
                self._auxscope.close()
            block = self.ctx.enter_context(self.nc.Block())
            for engname in ENGS:
                fns = self.prog[engname]
                getattr(block, engname)(lambda e, fns=fns: [f(e) for f in fns])
            self.ctx.close()
            self._finalized = self.nc
        return self._finalized

    def make_maps(self, in_maps):
        full = []
        for core, m in enumerate(in_maps):
            m = dict(m)
            for name, prov in self.providers.items():
                m[name] = prov(core)
            full.append(m)
        return full

    def run(self, in_maps, **kw):
        nc = self.finalize()
        return run_bass_kernel_spmd(nc, self.make_maps(in_maps),
                                    core_ids=list(range(8)), **kw)



def load_weight_direct(C, dram_ap, I, O, wb, woff=0, semname=None):
    IC = I // 128
    if woff == 0 and wb.shape[2] == O:
        dst = wb[:, 0:IC, :]
    else:
        dst = wb[:, 0:IC, woff:woff + O]
    return C.dma("gpsimd", dst, dram_ap.rearrange("(a p) o -> p a o", p=128),
                 semname=semname)


def load_weight_T(C, dram_ap, O, I, nat, wb, evac_eng="vector", woff=0):
    OC = O // 128
    IC = I // 128
    C.sem("s_tp")
    C.sem("s_tpe")
    C.w("gpsimd", "s_tp", C.cnt("s_tp"))
    nv = nat[:, 0:OC * I].rearrange("p (a i) -> p a i", a=OC)
    tok = C.dma("gpsimd", nv, dram_ap.rearrange("(a p) i -> p a i", p=128))
    C.wtok("tensor", tok)
    for idx in range(OC * IC):
        oc, ic = divmod(idx, IC)
        j = C.ct["s_tp"].n
        off = (j % 2) * 1024 + ((j // 2) % 8) * 128
        if j >= 1:
            C.w("tensor", "s_tpe", j - 1)
        C.do("tensor", lambda e, oc=oc, ic=ic, off=off: e.transpose(
            C.psT[:, off:off + 128],
            nv[:, oc, ic * 128:(ic + 1) * 128], C.id16[:]), inc=("s_tp", 1))
        C.w(evac_eng, "s_tp", C.ct["s_tp"].n)
        v = C.do(evac_eng, lambda e, oc=oc, ic=ic, off=off: e.tensor_copy(
            wb[:, ic, woff + oc * 128:woff + (oc + 1) * 128],
            C.psT[:, off:off + 128]), inc=("s_tpe", 1))
    return ("s_tpe", v)



def pe_transpose(C, src_ap):
    C.sem("s_tp"); C.sem("s_tpe")
    j = C.ct["s_tp"].n
    off = (j % 2) * 1024 + ((j // 2) % 8) * 128
    if j >= 1:
        C.w("tensor", "s_tpe", j - 1)
    C.do("tensor", lambda e, src_ap=src_ap, off=off: e.transpose(
        C.psT[:, off:off + 128], src_ap, C.id16[:]), inc=("s_tp", 1))
    return C.psT[:, off:off + 128], C.ct["s_tp"].n


def tp_consume(C, eng, fns, tp_val):
    C.w(eng, "s_tp", tp_val)
    for j, fn in enumerate(fns):
        if j == len(fns) - 1:
            C.do(eng, fn, inc=("s_tpe", 1))
        else:
            C.do(eng, fn)


def ln_rows(C, x, T, double=False, y_coeffs=False):
    row = C.row
    NT = T // TG
    for g in range(NT):
        sl = slice(g * TG, (g + 1) * TG)
        C.aux_guard()
        for c in range(DC):
            C.do("tensor", lambda e, c=c, sl=sl: e.matmul(
                C.psU[0:1, :], C.ones_col[:, 0:1], x[:, c, sl],
                start=(c == 0), stop=(c == DC - 1)),
                inc=("s_amm", 1) if c == DC - 1 else None)
        C.aux_evac("vector", [lambda e, g=g: e.tensor_copy(
            row[0:1, 0, g * TG:(g + 1) * TG], C.psU[0:1, :])])
        for half in range(2):
            if half == 0:
                C.w("scalar", "s_amm", C.cnt("s_amm"))
            else:
                C.w("scalar", "s_xw", C.cnt("s_xw"))
            for c4 in range(4):
                c = half * 4 + c4
                C.do("scalar", lambda e, c=c, c4=c4, sl=sl: e.activation(
                    C.xsq[:, c4, :], x[:, c, sl], AF.Square),
                    inc=("s_sq", 1) if c4 == 3 else None)
            C.w("tensor", "s_sq", C.cnt("s_sq"))
            if half == 0:
                C.aux_guard()
            for c4 in range(4):
                last_of_group = (half == 1 and c4 == 3)
                C.do("tensor", lambda e, c4=c4, half=half: e.matmul(
                    C.psU[0:1, :], C.ones_col[:, 0:1], C.xsq[:, c4, :],
                    start=(half == 0 and c4 == 0), stop=(half == 1 and c4 == 3)),
                    inc=("s_amm", 1) if last_of_group else ("s_xw", 1) if c4 == 3 else None)
        C.aux_evac("vector", [lambda e, g=g: e.tensor_copy(
            row[0:1, 4, g * TG:(g + 1) * TG], C.psU[0:1, :])])
    rT = slice(0, T)
    C.do("vector", lambda e: e.tensor_scalar(row[0:1, 1, rT], row[0:1, 0, rT],
                                             1.0 / D, None, ALU.mult))
    C.do("vector", lambda e: e.scalar_tensor_tensor(
        row[0:1, 0, rT], row[0:1, 1, rT], -1.0, row[0:1, 1, rT], ALU.mult, ALU.mult))
    C.do("vector", lambda e: e.scalar_tensor_tensor(
        row[0:1, 0, rT], row[0:1, 4, rT], 1.0 / D, row[0:1, 0, rT], ALU.mult, ALU.add),
        inc=("s_x", 1))
    C.do("vector", lambda e: e.tensor_scalar(row[0:1, 2, rT], row[0:1, 0, rT],
                                             EPS, None, ALU.add), inc=("s_x", 1))
    C.w("scalar", "s_x", C.cnt("s_x"))
    C.do("scalar", lambda e: e.activation(row[0:1, 2, rT], row[0:1, 2, rT], AF.Ln))
    C.do("scalar", lambda e: e.activation(row[0:1, 2, rT], row[0:1, 2, rT], AF.Exp, scale=-0.5),
         inc=("s_x", 1))
    C.w("vector", "s_x", C.cnt("s_x"))
    if double:
        if y_coeffs:
            C.do("vector", lambda e: e.tensor_copy(row[0:1, 4, rT], row[0:1, 2, rT]))
        C.do("vector", lambda e: e.tensor_mul(row[0:1, 3, rT], row[0:1, 2, rT], row[0:1, 2, rT]))
        C.do("vector", lambda e: e.tensor_mul(row[0:1, 3, rT], row[0:1, 3, rT], row[0:1, 0, rT]))
        C.do("vector", lambda e: e.tensor_scalar(row[0:1, 3, rT], row[0:1, 3, rT],
                                                 EPS, None, ALU.add), inc=("s_x", 1))
        C.w("scalar", "s_x", C.cnt("s_x"))
        C.do("scalar", lambda e: e.activation(row[0:1, 3, rT], row[0:1, 3, rT], AF.Ln))
        C.do("scalar", lambda e: e.activation(row[0:1, 3, rT], row[0:1, 3, rT], AF.Exp, scale=-0.5),
             inc=("s_x", 1))
        C.w("vector", "s_x", C.cnt("s_x"))
        C.do("vector", lambda e: e.tensor_mul(row[0:1, 2, rT], row[0:1, 2, rT], row[0:1, 3, rT]))
    C.do("vector", lambda e: e.scalar_tensor_tensor(
        row[0:1, 3, rT], row[0:1, 1, rT], -1.0, row[0:1, 2, rT], ALU.mult, ALU.mult),
        inc=("s_row", 1))
    return row


def ln_apply(C, x, row, T, out, lnw_row=None, lnb_row=None, extra=None):
    C.w("tensor", "s_row", C.cnt("s_row"))
    NT = T // TG
    for c in range(DC):
        for g in range(NT):
            sl = slice(g * TG, (g + 1) * TG)
            C.aux_guard()
            if lnw_row is not None:
                C.do("tensor", lambda e, c=c, sl=sl: e.matmul(
                    C.psS[:, :], lnw_row[0:1, c, :], row[0:1, 2, sl], start=True, stop=True))
                C.do("tensor", lambda e, c=c, sl=sl: e.matmul(
                    C.psO[:, :], lnw_row[0:1, c, :], row[0:1, 3, sl], start=True, stop=False))
                C.do("tensor", lambda e, c=c, sl=sl: e.matmul(
                    C.psO[:, :], lnb_row[0:1, c, :], C.onesT[0:1, sl], start=False, stop=True),
                    inc=("s_amm", 1))
            else:
                C.do("tensor", lambda e, sl=sl: e.matmul(
                    C.psS[:, :], C.ones128[0:1, :], row[0:1, 2, sl], start=True, stop=True))
                C.do("tensor", lambda e, sl=sl: e.matmul(
                    C.psO[:, :], C.ones128[0:1, :], row[0:1, 3, sl], start=True, stop=True),
                    inc=("s_amm", 1))
            fns = [
                (lambda e, c=c, sl=sl: e.tensor_mul(C.t_ln[:, :], x[:, c, sl], C.psS[:, :])),
                (lambda e, c=c, sl=sl: e.tensor_add(out[:, c, sl], C.t_ln[:, :], C.psO[:, :])),
            ]
            if extra is not None:
                fns = fns + extra(c, g, sl)
            C.aux_evac("vector", fns)


def ln_normed(C, x, T, double, lnw_row=None, lnb_row=None, out=None, extra=None,
              y_coeffs=False):
    row = ln_rows(C, x, T, double=double, y_coeffs=y_coeffs)
    ln_apply(C, x, row, T, out, lnw_row=lnw_row, lnb_row=lnb_row, extra=extra)
    return row



def gemm(C, wb_sel, xb, KC, DOT, T, evac_fn, evac_eng="vector", TGT=TG, tg_outer=True,
         mm_dep_fn=None, dep_fn=None, evsem=None):
    if evsem is None:
        C.uid += 1
        evsem = f"s_ge{C.uid}"
    NT = T // TGT
    toks = {}
    outer, inner = (range(NT), range(DOT)) if tg_outer else (range(DOT), range(NT))
    for a in outer:
        for b in inner:
            g, do = (a, b) if tg_outer else (b, a)
            fns = [
                (lambda e, ps, first, last, k=k, do=do, g=g: e.matmul(
                    ps[:, 0:TGT], wb_sel(k, do),
                    xb[:, k, g * TGT:(g + 1) * TGT], start=first, stop=last))
                for k in range(KC)
            ]
            ps = C.rot_tile(fns, TGT=TGT,
                            mm_dep=mm_dep_fn(do, g) if mm_dep_fn else None)
            toks[(do, g)] = C.rot_evac(evac_eng, evac_fn(do, g, ps), evsem=evsem,
                                       dep=dep_fn(do, g) if dep_fn else None)
    return toks



def emit_l0ffn(C, x, T, wg_ap, wu_ap, wd_ap, NS=6):
    s = 1.0 / 127.0
    HS = 512
    SC = HS // 128
    nx = C.sb("ffn_nx", [128, DC, T], BF16)
    gbuf = C.sb("ffn_g", [128, SC, T], BF16)
    nat = C.sb("ffn_nat", [128, 4096], BF16)
    wT = C.sb("ffn_wT", [128, DC, HS], BF16)
    wdT = C.sb("ffn_wdT", [128, SC, 1024], BF16)
    ln_normed(C, x, T, double=False, out=nx)
    nx_tok = ("s_aev", C.cnt("s_aev"))
    for sidx in range(NS):
        hsl = slice(sidx * HS, (sidx + 1) * HS)
        wg_tok = load_weight_T(C, wg_ap[hsl, :], HS, D, nat, wT)
        def ev_gate(do, g, ps):
            return [lambda e, do=do, g=g, ps=ps: e.activation(
                gbuf[:, do, g * TG:(g + 1) * TG], ps[:, :], AF.Silu, scale=s)]
        gate_toks = gemm(C, lambda k, do: wT[:, k, do * 128:(do + 1) * 128], nx, DC, SC, T,
                         ev_gate, evac_eng="scalar",
                         mm_dep_fn=lambda do, g, t=(nx_tok, wg_tok): max_tok(*t))
        wu_tok = load_weight_T(C, wu_ap[hsl, :], HS, D, nat, wT)
        def ev_up(do, g, ps):
            return [lambda e, do=do, g=g, ps=ps: e.scalar_tensor_tensor(
                gbuf[:, do, g * TG:(g + 1) * TG], ps[:, :], s,
                gbuf[:, do, g * TG:(g + 1) * TG], ALU.mult, ALU.mult)]
        up_toks = gemm(C, lambda k, do: wT[:, k, do * 128:(do + 1) * 128], nx, DC, SC, T,
                       ev_up, evac_eng="vector",
                       mm_dep_fn=lambda do, g, t=wu_tok: t,
                       dep_fn=lambda do, g, gt=gate_toks: gt[(do, g)])
        wd_tok = load_weight_T(C, wd_ap[:, hsl], D, HS, nat, wdT)
        def ev_down(do, g, ps):
            return [lambda e, do=do, g=g, ps=ps: e.scalar_tensor_tensor(
                x[:, do, g * TG:(g + 1) * TG], ps[:, :], s,
                x[:, do, g * TG:(g + 1) * TG], ALU.mult, ALU.add)]
        gemm(C, lambda k, do: wdT[:, k, do * 128:(do + 1) * 128], gbuf, SC, DC, T,
             ev_down, evac_eng="vector",
             mm_dep_fn=lambda do, g, wt=wd_tok, ut=up_toks: max_tok(wt, ut[(SC - 1, g)]))



def stream_in(C, T=2048):
    x = C.sb("stream", [128, DC, T], F32)
    xd = C.inp("x_in", [D, T], F32)
    tok = C.dma("sync", x[:], xd.rearrange("(c p) t -> p c t", p=128))
    for e in ("tensor", "vector", "scalar", "gpsimd"):
        C.wtok(e, tok)
    return x


def full_barrier_on(C, eng):
    for sname in list(C.ct):
        C.w(eng, sname, C.cnt(sname))


def stream_out(C, x, T=2048):
    xo = C.out("x_out", [D, T], F32)
    full_barrier_on(C, "sync")
    tok = C.dma("sync", xo.rearrange("(c p) t -> p c t", p=128), x[:])
    C.wtok("sync", tok)


def build_l0ffn_launch(T=2048):
    C = Launch("l0ffn")
    C.setup_common(maxT=T)
    x = stream_in(C, T)
    wg = C.inp("w_gate", [3072, 1024], I32)
    wu = C.inp("w_up", [3072, 1024], I32)
    wd = C.inp("w_down", [1024, 3072], I32)
    emit_l0ffn(C, x, T, wg, wu, wd)
    stream_out(C, x, T)
    return C



def emit_l1ffn(C, x_dram, xout_dram, T, wg_ap, bgbc_ap, w1_ap, b1t_ap, w2_ap,
               b2bc_ap, gbc_ap, bbc_ap, NE=8, stop_after=None):
    NTT = T // 128
    nx = C.sb("m_nx", [128, DC, T], BF16)
    lg = C.sb("m_lg", [128, NTT * 8], F32)
    cw = C.sb("m_cw", [128, NTT * 8], F32)
    C._dbg_lg = lg; C._dbg_cw = cw
    b1sb = C.sb("m_b1", [128, 32, NE], F32)
    b2sb = C.sb("m_b2", [128, NE, 1024], BF16)
    gbc = C.sb("m_gbc", [128, 1024], F32)
    bbc = C.sb("m_bbc", [128, 1024], F32)
    wgb = C.sb("m_wg", [128, DC, 8], F32)
    acc_d = C.dram("m_accd", [T, 1024], F32)
    C.make_arena(96)
    xbuf = C.carve(0, [128, DC, T], F32)
    Wx = C.carve(80.0, [128, NTT * 8], F32)
    W2x = C.carve(80.5, [128, NTT * 8], F32)
    BUF2 = C.carve(81.0, [128, NTT * 8], F32)
    Mbc = W2x
    Rbc = BUF2
    eqx = BUF2
    gex = BUF2
    st6 = C.carve(81.5, [128, 6, NTT], F32)
    nm16 = st6[:, 0, :]
    m116 = st6[:, 1, :]
    m216 = st6[:, 2, :]
    dn16 = st6[:, 3, :]
    dn16b = st6[:, 4, :]
    rd16 = st6[:, 5, :]
    on8 = C.carve(81.875, [128, 8], F32)
    bgbc = C.carve(82.0, [128, NTT * 8], F32)
    xtok = C.dma("sync", xbuf, x_dram.rearrange("(c p) t -> p c t", p=128))
    for e in ("tensor", "vector", "scalar", "gpsimd"):
        C.wtok(e, xtok)
    row = ln_normed(C, xbuf, T, double=False, out=nx)
    nx_tok = ("s_aev", C.cnt("s_aev"))
    Rcol = C.sb("m_Rcol", [128, NTT], F32)
    Mcol = C.sb("m_Mcol", [128, NTT], F32)
    C._dbg_Rcol = Rcol; C._dbg_Mcol = Mcol
    C.w("tensor", "s_row", C.cnt("s_row"))
    C.aux_guard()
    for i in range(NTT):
        C.do("tensor", lambda e, i=i: e.matmul(
            C.psU[:, i:i + 1], row[0:1, 2, i * 128:(i + 1) * 128],
            C.ones_col[0:1, 0:1], start=True, stop=True))
        C.do("tensor", lambda e, i=i: e.matmul(
            C.psU[:, NTT + i:NTT + i + 1], row[0:1, 1, i * 128:(i + 1) * 128],
            C.ones_col[0:1, 0:1], start=True, stop=True),
            inc=("s_amm", 1) if i == NTT - 1 else None)
    C.aux_evac("vector", [
        (lambda e: e.tensor_copy(Rcol[:], C.psU[:, 0:NTT])),
        (lambda e: e.tensor_copy(Mcol[:], C.psU[:, NTT:2 * NTT])),
    ])
    t_rc = ("s_aev", C.cnt("s_aev"))
    t_rc2 = t_rc
    t_b1 = C.dma("sync", b1sb[:, :, 0:NE], b1t_ap.rearrange("(a p) e -> p a e", p=128))
    t_b2 = C.dma("gpsimd", b2sb[:], b2bc_ap)
    t_bg = C.dma("sync", bgbc, bgbc_ap)
    t_g = C.dma("sync", gbc[:], gbc_ap)
    t_b = C.dma("sync", bbc[:], bbc_ap)
    t_wg = C.dma("sync", wgb[:], wg_ap.rearrange("(a p) o -> p a o", p=128))
    C.aux_guard()
    C.wtok("tensor", t_wg)
    for k in range(DC):
        C.do("tensor", lambda e, k=k: e.matmul(
            C.psU[0:1, 0:8], C.ones_col[:, 0:1], wgb[:, k, 0:8],
            start=(k == 0), stop=(k == DC - 1)), inc=("s_amm", 1) if k == DC - 1 else None)
    csum = C.sb("m_csum", [1, 8], F32)
    csbc = C.sb("m_csbc", [128, 8], F32)
    C._dbg_csbc = csbc
    C.aux_evac("vector", [lambda e: e.tensor_scalar(csum[0:1, :], C.psU[0:1, 0:8],
                                                    -1.0, None, ALU.mult)],)
    C.w("tensor", "s_aev", C.cnt("s_aev"))
    C.aux_guard()
    C.do("tensor", lambda e: e.matmul(C.psU[:, 0:8], C.ones128[0:1, :], csum[0:1, :],
                                      start=True, stop=True), inc=("s_amm", 1))
    C.aux_evac("vector", [lambda e: e.tensor_copy(csbc[:], C.psU[:, 0:8])])
    C.do("gpsimd", lambda e: e.memset(on8, 1.0), inc=("s_pr", 1))
    C.w("tensor", "s_aev", C.cnt("s_aev"))
    for i in range(NTT):
        for k in range(DC):
            C.do("tensor", lambda e, k=k, i=i: e.matmul(
                C.psR[:, i * 8:(i + 1) * 8], xbuf[:, k, i * 128:(i + 1) * 128],
                wgb[:, k, 0:8], start=(k == 0), stop=(k == DC - 1)),
                inc=("s_rmm", 1) if k == DC - 1 else None)
    C.w("vector", "s_rmm", C.cnt("s_rmm"))
    C.w("vector", "s_pr", C.cnt("s_pr"))
    C.wtok("vector", t_bg)
    C.do("vector", lambda e: e.tensor_copy(lg[:], C.psR[:, 0:NTT * 8]),
         inc=("s_lgev", 1))
    lgev_tok = ("s_lgev", C.cnt("s_lgev"))
    for i in range(NTT):
        sl = slice(i * 8, (i + 1) * 8)
        C.do("vector", lambda e, sl=sl, i=i: e.tensor_scalar(
            Mbc[:, sl], csbc[:], Mcol[:, i:i + 1], None, ALU.mult))
        C.do("vector", lambda e, sl=sl, i=i: e.tensor_scalar(
            Rbc[:, sl], on8, Rcol[:, i:i + 1], None, ALU.mult))
    C.do("vector", lambda e: e.tensor_add(lg[:], lg[:], Mbc[:]))
    C.do("vector", lambda e: e.tensor_mul(lg[:], lg[:], Rbc[:]))
    C.do("vector", lambda e: e.tensor_add(lg[:], lg[:], bgbc))
    C.do("vector", lambda e: e.memset(m116, 0.0))
    C.do("vector", lambda e: e.memset(m216, 0.0))
    C._dbg_W = Wx
    TS8 = [slice(i * 8, (i + 1) * 8) for i in range(NTT)]
    for i, sl in enumerate(TS8):
        C.do("vector", lambda e, sl=sl, i=i: e.reduce_max(
            nm16[:, i:i + 1], lg[:, sl], AX.X, negate=True),
            inc=("s_gv", 1) if i == NTT - 1 else None)
    C.w("scalar", "s_gv", C.cnt("s_gv"))
    for i, sl in enumerate(TS8):
        C.do("scalar", lambda e, sl=sl, i=i: e.activation(
            Wx[:, sl], lg[:, sl], AF.Exp, bias=nm16[:, i:i + 1]),
            inc=("s_gs", 1) if i == NTT - 1 else None)
    C.w("vector", "s_gs", C.cnt("s_gs"))
    for i, sl in enumerate(TS8):
        C.do("vector", lambda e, sl=sl, i=i: e.reduce_max(m116[:, i:i + 1], Wx[:, sl], AX.X))
    for i, sl in enumerate(TS8):
        C.do("vector", lambda e, sl=sl, i=i: e.tensor_scalar(
            eqx[:, sl], Wx[:, sl], m116[:, i:i + 1], None, ALU.is_equal))
    for i, sl in enumerate(TS8):
        C.do("vector", lambda e, sl=sl: e.scalar_tensor_tensor(
            W2x[:, sl], eqx[:, sl], -1e30, Wx[:, sl], ALU.mult, ALU.add))
    for i, sl in enumerate(TS8):
        C.do("vector", lambda e, sl=sl, i=i: e.reduce_max(m216[:, i:i + 1], W2x[:, sl], AX.X))
    for i in range(NTT):
        C.do("vector", lambda e, i=i: e.tensor_add(
            dn16[:, i:i + 1], m116[:, i:i + 1], m216[:, i:i + 1]))
    for i in range(NTT):
        C.do("vector", lambda e, i=i: e.tensor_scalar(
            dn16b[:, i:i + 1], dn16[:, i:i + 1], 1e-8, None, ALU.add))
    for i in range(NTT):
        C.do("vector", lambda e, i=i: e.reciprocal(rd16[:, i:i + 1], dn16b[:, i:i + 1]))
    for i, sl in enumerate(TS8):
        C.do("vector", lambda e, sl=sl, i=i: e.tensor_scalar(
            gex[:, sl], Wx[:, sl], m216[:, i:i + 1], None, ALU.is_ge))
    for i, sl in enumerate(TS8):
        C.do("vector", lambda e, sl=sl: e.tensor_mul(W2x[:, sl], Wx[:, sl], gex[:, sl]))
    for i, sl in enumerate(TS8):
        C.do("vector", lambda e, sl=sl, i=i: e.tensor_scalar(
            cw[:, sl], W2x[:, sl], rd16[:, i:i + 1], None, ALU.mult),
            inc=("s_cw", 1) if i == NTT - 1 else None)
    cw_tok = ("s_cw", C.cnt("s_cw"))
    if stop_after == "gating":
        full_barrier_on(C, "sync")
        tok = C.dma("sync", xout_dram[:, 0:128].rearrange("(c p) t -> p c t", p=128),
                    xbuf[:, :, 0:128])
        C.wtok("sync", tok)
        return
    w1b = C.carve(0, [128, DC, 2048], BF16)
    w2b = C.carve(32, [128, 16, 1024], BF16)
    Hb = C.carve(64, [128, 16, TG], BF16)
    outsb = C.carve(80, [128, 4, 1024], F32)
    C.sem("s_fill")
    slot_uses = [0, 0, 0, 0]
    for e_i in range(NE):
        for half in range(2):
            C.w("gpsimd", "s_rmm", C.cnt("s_rmm"))
            t_w1 = C.dma("gpsimd", w1b,
                         w1_ap[e_i, :, half * 2048:(half + 1) * 2048]
                         .rearrange("(a p) o -> p a o", p=128), semname="s_wl")
            t_w2 = C.dma("gpsimd", w2b,
                         w2_ap[e_i, half * 2048:(half + 1) * 2048, :]
                         .rearrange("(a p) o -> p a o", p=128), semname="s_wl")
            for tw in range(T // TG):
                for ht in range(16):
                    hg = half * 16 + ht
                    fns = [(lambda e, ps, first, last, k=k, ht=ht, tw=tw: e.matmul(
                        ps[:, :], w1b[:, k, ht * 128:(ht + 1) * 128],
                        nx[:, k, tw * TG:(tw + 1) * TG], start=first, stop=last))
                        for k in range(DC)]
                    ps = C.rot_tile(fns, mm_dep=max_tok(t_w1, lgev_tok)
                                    if ht == 0 and tw == 0 else None)
                    C.rot_evac("scalar", [
                        lambda e, ps=ps, ht=ht, hg=hg, e_i=e_i: e.activation(
                            Hb[:, ht, :], ps[:, :], AF.Gelu,
                            bias=b1sb[:, hg, e_i:e_i + 1])],
                        dep=t_b1 if e_i == 0 and half == 0 and tw == 0 and ht == 0 else None)
                h_done = ("s_rv_scalar", C.cnt("s_rv_scalar"))
                for tt in range(4):
                    gtt = tw * 4 + tt
                    slot = gtt % 4
                    for dh in range(2):
                        fns = [(lambda e, ps, first, last, k=k, tt=tt, dh=dh: e.matmul(
                            ps[:, :], Hb[:, k, tt * 128:(tt + 1) * 128],
                            w2b[:, k, dh * 512:(dh + 1) * 512], start=first, stop=last))
                            for k in range(16)]
                        ps = C.rot_tile(fns, mm_dep=max_tok(h_done, t_w2)
                                        if tt == 0 and dh == 0 else None)
                        deps = [("s_ac%d" % slot, 16 * slot_uses[slot])] if dh == 0 and slot_uses[slot] else []
                        if e_i == 0 and half == 0 and gtt < 4:
                            deps.append(cw_tok)
                        C.rot_evac("vector", [
                            lambda e, ps=ps, slot=slot, dh=dh, gtt=gtt, e_i=e_i: e.tensor_scalar(
                                outsb[:, slot, dh * 512:(dh + 1) * 512], ps[:, :],
                                cw[:, gtt * 8 + e_i:gtt * 8 + e_i + 1], None, ALU.mult)],
                            dep=max_tok(*deps) if deps else None)
                    if half == 0:
                        for dh in range(2):
                            C.do("vector", lambda e, slot=slot, dh=dh, gtt=gtt, e_i=e_i:
                                 e.scalar_tensor_tensor(
                                     outsb[:, slot, dh * 512:(dh + 1) * 512],
                                     b2sb[:, e_i, dh * 512:(dh + 1) * 512],
                                     cw[:, gtt * 8 + e_i:gtt * 8 + e_i + 1],
                                     outsb[:, slot, dh * 512:(dh + 1) * 512],
                                     ALU.mult, ALU.add),
                                 inc=("s_fill", 1) if dh == 1 else None)
                    else:
                        C.do("vector", lambda e: e.engine_nop(), inc=("s_fill", 1))
                    C.w("gpsimd", "s_fill", C.cnt("s_fill"))
                    first_acc = (e_i == 0 and half == 0)
                    C.dma("gpsimd", acc_d[gtt * 128:(gtt + 1) * 128, :],
                          outsb[:, slot, :], semname="s_ac%d" % slot,
                          accum_op=ALU.bypass if first_acc else ALU.add)
                    slot_uses[slot] += 1
    for sl in range(4):
        C.w("vector", "s_ac%d" % sl, 16 * slot_uses[sl])
    accb = C.carve(0, [128, NTT, 1024], F32)
    jk = C.carve(64, [128, 1024], F32)
    ybf = C.carve(68, [128, 1024], BF16)
    xsl = C.carve(72, [128, DC, 128], F32)
    stm = C.carve(78, [128, 8, NTT], F32)
    C.w("sync", "s_rmm", C.cnt("s_rmm"))
    for sl in range(4):
        C.w("sync", "s_ac%d" % sl, 16 * slot_uses[sl])
    t_acc = C.dma("sync", accb, acc_d.rearrange("(a p) d -> p a d", p=128))
    C.wtok("vector", t_acc)
    C.wtok("scalar", t_acc)
    for i in range(NTT):
        C.do("vector", lambda e, i=i: e.reduce_sum(stm[:, 0, i:i + 1], accb[:, i, :], AX.X))
    for i in range(NTT):
        C.do("vector", lambda e, i=i: e.tensor_scalar(
            stm[:, 1, i:i + 1], stm[:, 0, i:i + 1], 1.0 / D, None, ALU.mult))
    for i in range(NTT):
        C.do("vector", lambda e, i=i: e.tensor_mul(
            stm[:, 3, i:i + 1], stm[:, 1, i:i + 1], stm[:, 1, i:i + 1]))
    for i in range(NTT):
        C.do("scalar", lambda e, i=i: e.activation(
            jk, accb[:, i, :], AF.Square, accum_out=stm[:, 2, i:i + 1]),
            inc=("s_ps", 1) if i == NTT - 1 else None)
    C.w("vector", "s_ps", C.cnt("s_ps"))
    for i in range(NTT):
        C.do("vector", lambda e, i=i: e.scalar_tensor_tensor(
            stm[:, 4, i:i + 1], stm[:, 2, i:i + 1], 1.0 / D, stm[:, 3, i:i + 1],
            ALU.mult, ALU.subtract))
    for i in range(NTT):
        C.do("vector", lambda e, i=i: e.tensor_scalar(
            stm[:, 5, i:i + 1], stm[:, 4, i:i + 1], EPS, None, ALU.add),
            inc=("s_pv", 1) if i == NTT - 1 else None)
    C.w("scalar", "s_pv", C.cnt("s_pv"))
    for i in range(NTT):
        C.do("scalar", lambda e, i=i: e.activation(
            stm[:, 6, i:i + 1], stm[:, 5, i:i + 1], AF.Ln))
    for i in range(NTT):
        C.do("scalar", lambda e, i=i: e.activation(
            stm[:, 7, i:i + 1], stm[:, 6, i:i + 1], AF.Exp, scale=-0.5),
            inc=("s_ps", 1) if i == NTT - 1 else None)
    C.w("vector", "s_ps", C.cnt("s_ps"))
    prev_store = None
    for i in range(NTT):
        A = accb[:, i, :]
        C.do("vector", lambda e, A=A, i=i: e.tensor_scalar(
            jk, A, stm[:, 1, i:i + 1], stm[:, 7, i:i + 1], ALU.subtract, ALU.mult))
        if i == 0:
            C.wtok("vector", t_g)
            C.wtok("vector", t_b)
        C.do("vector", lambda e: e.tensor_mul(jk, jk, gbc[:]))
        C.do("vector", lambda e: e.tensor_add(ybf, jk, bbc[:]), inc=("s_yb", 1))
        t_x = C.dma("sync", xsl, x_dram[:, i * 128:(i + 1) * 128]
                    .rearrange("(c p) t -> p c t", p=128), semname="s_xl")
        C.w("tensor", "s_yb", C.cnt("s_yb"))
        for c in range(DC):
            ps_ap, tpv = pe_transpose(C, ybf[:, c * 128:(c + 1) * 128])
            if c == 0:
                C.wtok("vector", t_x)
                if prev_store is not None:
                    C.wtok("vector", prev_store)
            tp_consume(C, "vector", [
                lambda e, ps_ap=ps_ap, c=c: e.scalar_tensor_tensor(
                    xsl[:, c, :], ps_ap, 1.0, xsl[:, c, :], ALU.mult, ALU.add)], tpv)
        C.do("vector", lambda e: e.engine_nop(), inc=("s_xa", 1))
        C.w("sync", "s_xa", C.cnt("s_xa"))
        prev_store = C.dma("sync", xout_dram[:, i * 128:(i + 1) * 128]
                           .rearrange("(c p) t -> p c t", p=128), xsl, semname="s_xs")
    C.wtok("sync", prev_store)



def emit_l2ffn(C, x, T, win_ap, bincol_ap, Econst_ap, w1col_ap, b1col_ap,
               blkw2_ap, b2col_ap, wout_ap, boutcol_ap):
    nx = C.sb("f2_nx", [128, DC, T], BF16)
    winb = C.sb("f2_win", [128, DC, 64], BF16)
    Eb = C.sb("f2_E", [64, 1024], BF16)
    bincol = C.sb("f2_bin", [64, 1], F32)
    w1col = C.sb("f2_w1", [128, 8], F32)
    b1col = C.sb("f2_b1", [128, 8], F32)
    blkw2 = C.sb("f2_w2", [128, 8, 64], BF16)
    b2col = C.sb("f2_b2", [64, 1], F32)
    woutb = C.sb("f2_wo", [64, 1024], BF16)
    boutcol = C.sb("f2_bo", [128, 8], F32)
    sfm = C.sb("f2_s", [64, T], BF16)
    hfm = C.sb("f2_h", [128, 8, T], BF16)
    gfm = C.sb("f2_g", [64, T], BF16)
    C._f2s, C._f2h, C._f2g = sfm, hfm, gfm
    ln_normed(C, x, T, double=False, out=nx)
    nx_tok = ("s_aev", C.cnt("s_aev"))
    t1 = C.dma("gpsimd", winb[:], win_ap.rearrange("(a p) o -> p a o", p=128))
    t2 = C.dma("gpsimd", Eb[:], Econst_ap)
    t3 = C.dma("sync", bincol[:], bincol_ap)
    t4 = C.dma("sync", w1col[:], w1col_ap)
    t5 = C.dma("sync", b1col[:], b1col_ap)
    t6 = C.dma("gpsimd", blkw2[:], blkw2_ap.rearrange("c p m -> p c m"))
    t7 = C.dma("sync", b2col[:], b2col_ap)
    t8 = C.dma("gpsimd", woutb[:], wout_ap)
    t9 = C.dma("sync", boutcol[:], boutcol_ap)
    NT = T // TG
    for g in range(NT):
        fns = [(lambda e, ps, first, last, k=k, g=g: e.matmul(
            ps[0:64, :], winb[:, k, :], nx[:, k, g * TG:(g + 1) * TG],
            start=first, stop=last)) for k in range(DC)]
        ps = C.rot_tile(fns, mm_dep=max_tok(nx_tok, t1) if g == 0 else None)
        C.rot_evac("vector", [lambda e, g=g, ps=ps: e.tensor_scalar(
            sfm[0:64, g * TG:(g + 1) * TG], ps[0:64, :], bincol[0:64, 0:1],
            None, ALU.add)], dep=t3 if g == 0 else None)
    for c in range(DC):
        for g in range(NT):
            fns = [(lambda e, ps, first, last, c=c, g=g: e.matmul(
                ps[:, :], Eb[0:64, c * 128:(c + 1) * 128],
                sfm[0:64, g * TG:(g + 1) * TG], start=first, stop=last))]
            ps = C.rot_tile(fns, mm_dep=t2 if c == 0 and g == 0 else None)
            C.rot_evac("scalar", [lambda e, c=c, g=g, ps=ps: e.activation(
                hfm[:, c, g * TG:(g + 1) * TG], ps[:, :], AF.Relu,
                bias=b1col[:, c:c + 1], scale=w1col[:, c:c + 1])],
                dep=max_tok(t4, t5) if c == 0 and g == 0 else None)
    for g in range(NT):
        fns = [(lambda e, ps, first, last, k=k, g=g: e.matmul(
            ps[0:64, :], blkw2[:, k, :], hfm[:, k, g * TG:(g + 1) * TG],
            start=first, stop=last)) for k in range(DC)]
        ps = C.rot_tile(fns, mm_dep=t6 if g == 0 else None)
        C.rot_evac("vector", [lambda e, g=g, ps=ps: e.tensor_scalar(
            gfm[0:64, g * TG:(g + 1) * TG], ps[0:64, :], b2col[0:64, 0:1],
            None, ALU.add)], dep=t7 if g == 0 else None)
    for g in range(NT):
        for do in range(DC):
            fns = [(lambda e, ps, first, last, do=do, g=g: e.matmul(
                ps[:, :], woutb[0:64, do * 128:(do + 1) * 128],
                gfm[0:64, g * TG:(g + 1) * TG], start=first, stop=last))]
            ps = C.rot_tile(fns, mm_dep=t8 if g == 0 and do == 0 else None)
            C.rot_evac("vector", [lambda e, do=do, g=g, ps=ps: e.scalar_tensor_tensor(
                x[:, do, g * TG:(g + 1) * TG], ps[:, :], boutcol[:, do:do + 1],
                x[:, do, g * TG:(g + 1) * TG], ALU.add, ALU.add)],
                dep=t9 if g == 0 and do == 0 else None)


def l2ffn_host_inputs(z, pref="L2_ffn_"):
    W1 = z[pref + "W1"]; b1 = z[pref + "b1"]; W2 = z[pref + "W2"]
    E = np.zeros((64, 1024), np.float32)
    for m in range(64):
        E[m, m * 16:(m + 1) * 16] = 1.0
    return {
        "f2win": z[pref + "Win"], "f2bin": z[pref + "bin_"].reshape(64, 1),
        "f2E": E.astype(BF),
        "f2w1": np.ascontiguousarray(W1.reshape(1024).reshape(8, 128).T),
        "f2b1": np.ascontiguousarray(b1.reshape(1024).reshape(8, 128).T),
        "f2w2": blk_w2(W2),
        "f2b2": z[pref + "b2"].reshape(64, 1),
        "f2wo": z[pref + "Wout"], "f2bo": np.ascontiguousarray(
            z[pref + "bout"].reshape(8, 128).T),
    }


def blk_w2(W2):
    out = np.zeros((8, 128, 64), np.float32)
    flat = W2.reshape(1024)
    for mi in range(1024):
        c, p = divmod(mi, 128)
        out[c, p, mi // 16] = flat[mi]
    return out.astype(BF)


def l2ffn_declare(C):
    return dict(
        win_ap=C.inp("f2win", [1024, 64], F32),
        bincol_ap=C.inp("f2bin", [64, 1], F32),
        Econst_ap=C.inp("f2E", [64, 1024], BF16),
        w1col_ap=C.inp("f2w1", [128, 8], F32),
        b1col_ap=C.inp("f2b1", [128, 8], F32),
        blkw2_ap=C.inp("f2w2", [8, 128, 64], BF16),
        b2col_ap=C.inp("f2b2", [64, 1], F32),
        wout_ap=C.inp("f2wo", [64, 1024], F32),
        boutcol_ap=C.inp("f2bo", [128, 8], F32),
    )



def rope_mm(C, buf, tl, T, cosF, sinF, Pc, ra, dep=None):
    for g in range(T // TG):
        sl = slice(g * TG, (g + 1) * TG)
        fns = [(lambda e, ps, first, last, sl=sl: e.matmul(
            ps[:, :], Pc[:], buf[:, tl, sl], start=first, stop=last))]
        ps = C.rot_tile(fns, mm_dep=dep if g == 0 else None)
        C.rot_evac("vector", [
            (lambda e, ps=ps, sl=sl: e.tensor_mul(ra[:], ps[:, 0:TG], sinF[:, sl])),
            (lambda e, sl=sl: e.tensor_mul(buf[:, tl, sl], buf[:, tl, sl], cosF[:, sl])),
            (lambda e, sl=sl: e.tensor_add(buf[:, tl, sl], buf[:, tl, sl], ra[:])),
        ])


def emit_l3attn_qkv(C, x_dram, T, wq_ap, wk_ap, wv_ap, bqcol_ap, bkcol_ap, bvbc_ap,
                    cos_ap, sin_ap, qo_ap, ko_ap, vo_ap, TSB=1024):
    NSB = T // TSB
    xbuf = C.sb("a3_xb", [128, DC, TSB], F32)
    nx = C.sb("a3_nx", [128, DC, TSB], BF16)
    nat = C.sb("a3_nat", [128, 8192], BF16)
    wT = C.sb("a3_wT", [128, DC, 1024], BF16)
    qbuf = C.sb("a3_q", [128, DC, TSB], BF16)
    kbuf = C.sb("a3_k", [128, DC, TSB], BF16)
    vtm = C.sb("a3_v", [128, TSB // 128, 1024], BF16)
    bqcol = C.sb("a3_bq", [128, DC], F32)
    bkcol = C.sb("a3_bk", [128, DC], F32)
    bvbc = C.sb("a3_bv", [128, 1024], F32)
    cosF = C.sb("a3_cos", [128, T], BF16)
    sinF = C.sb("a3_sin", [128, T], BF16)
    Pc = C.sb("a3_P", [128, 128], BF16)
    ra = C.sb("a3_ra", [128, TG], BF16)
    tc = C.dma("gpsimd", cosF[:], cos_ap)
    ts_ = C.dma("gpsimd", sinF[:], sin_ap)
    P = np.zeros((128, 128), np.float32)
    for po in range(128):
        if po % 64 < 32:
            P[po + 32, po] = -1.0
        else:
            P[po - 32, po] = 1.0
    tpc = C.dma("gpsimd", Pc[:], C.const_in("a3_Pc", P.astype(BF)))
    tbq = C.dma("sync", bqcol[:], bqcol_ap)
    tbk = C.dma("sync", bkcol[:], bkcol_ap)
    tbv = C.dma("sync", bvbc[:], bvbc_ap)
    NT = TSB // TG
    prev = []
    for sb in range(NSB):
        tsl = slice(sb * TSB, (sb + 1) * TSB)
        C.w("sync", "s_rmm", C.cnt("s_rmm"))
        for tkn in prev:
            C.wtok("sync", tkn)
        prev = []
        xtok = C.dma("sync", xbuf[:], x_dram[:, tsl].rearrange("(c p) t -> p c t", p=128),
                     semname="s_x3")
        for e in ("tensor", "vector", "scalar", "gpsimd"):
            C.wtok(e, xtok)
        ln_normed(C, xbuf, TSB, double=False, out=nx)
        nx_tok = ("s_aev", C.cnt("s_aev"))
        for (w_ap, bcol, btok, obuf, onm) in ((wq_ap, bqcol, tbq, qbuf, qo_ap),
                                              (wk_ap, bkcol, tbk, kbuf, ko_ap)):
            wtok = load_weight_T(C, w_ap, 1024, 1024, nat, wT)
            for g in range(NT):
                for do in range(DC):
                    fns = [(lambda e, ps, first, last, k=k, do=do, g=g: e.matmul(
                        ps[:, :], wT[:, k, do * 128:(do + 1) * 128],
                        nx[:, k, g * TG:(g + 1) * TG], start=first, stop=last))
                        for k in range(DC)]
                    ps = C.rot_tile(fns, mm_dep=max_tok(nx_tok, wtok)
                                    if g == 0 and do == 0 else None)
                    C.rot_evac("vector", [lambda e, ps=ps, do=do, g=g, obuf=obuf, bcol=bcol:
                                          e.tensor_scalar(obuf[:, do, g * TG:(g + 1) * TG],
                                                          ps[:, :], bcol[:, do:do + 1],
                                                          None, ALU.add)],
                               dep=btok if g == 0 and do == 0 and sb == 0 else None)
            rdep = max_tok(tc, ts_, tpc) if sb == 0 else None
            prdone = ("s_rv_vector", C.cnt("s_rv_vector"))
            C.w("tensor", prdone[0], prdone[1])
            for tl in range(DC):
                rope_mm(C, obuf, tl, TSB, cosF[:, tsl], sinF[:, tsl], Pc, ra,
                        dep=rdep if tl == 0 else None)
            C.do("vector", lambda e: e.engine_nop(), inc=("s_q3", 1))
            C.w("sync", "s_q3", C.cnt("s_q3"))
            prev.append(C.dma("sync", onm[:, :, tsl], obuf[:], semname="s_o3"))
        wtok = load_weight_T(C, wv_ap, 1024, 1024, nat, wT)
        for tt in range(TSB // 128):
            for dh in range(2):
                fns = [(lambda e, ps, first, last, k=k, tt=tt, dh=dh: e.matmul(
                    ps[:, :], nx[:, k, tt * 128:(tt + 1) * 128],
                    wT[:, k, dh * 512:(dh + 1) * 512], start=first, stop=last))
                    for k in range(DC)]
                ps = C.rot_tile(fns, mm_dep=wtok if tt == 0 and dh == 0 else None)
                C.rot_evac("vector", [lambda e, ps=ps, tt=tt, dh=dh: e.tensor_add(
                    vtm[:, tt, dh * 512:(dh + 1) * 512], ps[:, :],
                    bvbc[:, dh * 512:(dh + 1) * 512])],
                    dep=tbv if tt == 0 and dh == 0 and sb == 0 else None)
        C.do("vector", lambda e: e.engine_nop(), inc=("s_q3", 1))
        C.w("sync", "s_q3", C.cnt("s_q3"))
        prev.append(C.dma("sync", vo_ap[:, sb * (TSB // 128):(sb + 1) * (TSB // 128), :],
                          vtm[:], semname="s_o3"))
    full_barrier_on(C, "sync")


def emit_l3attn_core(C, x_dram, xout_dram, T, TK, qf_ap, kf_ap, vf_ap, wo_ap,
                     bocol_ap):
    nat = C.sb("b3_nat", [128, 8192], BF16)
    woT = C.sb("b3_wo", [128, DC, 1024], BF16)
    q_h = C.sb("b3_qh", [64, T], BF16)
    k_h = C.sb("b3_kh", [64, TK], BF16)
    v_h = C.sb("b3_vh", [128, TK // 128, 64], BF16)
    Pb = C.sb("b3_P", [128, TK // 128, TG], BF16)
    C._b3Pb = Pb
    og = C.sb("b3_og", [128, DC, T], BF16)
    C._b3og = og; C._b3P = None
    sq = C.sb("b3_sq", [64, TK], BF16)
    rk = C.sb("b3_rk", [1, TK], F32)
    rq = C.sb("b3_rq", [1, T], F32)
    xsl3 = C.sb("b3_xsl", [128, DC, TG], F32)
    rdb = C.sb("b3_rdb", [64, TG], F32)
    nmqb = C.sb("b3_nmq", [1, T], BF16)
    bocol = C.sb("b3_bo", [128, DC], F32)
    tbo = C.dma("sync", bocol[:], bocol_ap)
    wotok = load_weight_T(C, wo_ap, 1024, 1024, nat, woT)
    NKT = TK // 128
    NT = T // TG
    for h in range(16):
        tl, po = h // 2, (h % 2) * 64
        C.w("sync", "s_rmm", C.cnt("s_rmm"))
        C.w("scalar", "s_rmm", C.cnt("s_rmm"))
        tq = C.dma("sync", q_h[:], qf_ap[po:po + 64, tl, :], semname="s_hl")
        tk = C.dma("sync", k_h[:], kf_ap[po:po + 64, tl, :], semname="s_hl")
        tv = C.dma("sync", v_h[:], vf_ap[:, :, h * 64:(h + 1) * 64], semname="s_hl")
        hl_tok = ("s_hl", C.cnt("s_hl"))
        C.w("scalar", "s_aev", C.cnt("s_aev"))
        C.wtok("scalar", hl_tok)
        C.do("scalar", lambda e: e.activation(sq[0:64, 0:TK], k_h[0:64, :], AF.Square),
             inc=("s_sq", 1))
        C.w("tensor", "s_sq", C.cnt("s_sq"))
        for a in range(TK // TG):
            C.aux_guard()
            C.do("tensor", lambda e, a=a: e.matmul(
                C.psU[0:1, :], C.ones_colb[0:64, 0:1], sq[0:64, a * TG:(a + 1) * TG],
                start=True, stop=True), inc=("s_amm", 1))
            C.aux_evac("vector", [lambda e, a=a: e.tensor_copy(
                rk[0:1, a * TG:(a + 1) * TG], C.psU[0:1, :])])
        knm3 = rdb[0:1, 0:1]
        C.do("vector", lambda e: e.reduce_max(knm3, rk[0:1, 0:TK], AX.X),
             inc=("s_gv", 1))
        C.w("scalar", "s_gv", C.cnt("s_gv"))
        C.do("scalar", lambda e: e.activation(sq[0:64, 0:T], q_h[0:64, :], AF.Square),
             inc=("s_sq", 1))
        C.w("tensor", "s_sq", C.cnt("s_sq"))
        for a in range(NT):
            C.aux_guard()
            C.do("tensor", lambda e, a=a: e.matmul(
                C.psU[0:1, :], C.ones_colb[0:64, 0:1], sq[0:64, a * TG:(a + 1) * TG],
                start=True, stop=True), inc=("s_amm", 1))
            C.aux_evac("vector", [lambda e, a=a: e.tensor_copy(
                rq[0:1, a * TG:(a + 1) * TG], C.psU[0:1, :])])
        C.do("vector", lambda e: e.tensor_scalar(rq[0:1, 0:T], rq[0:1, 0:T],
                                                 knm3, None, ALU.mult),
             inc=("s_gv", 1))
        C.w("scalar", "s_gv", C.cnt("s_gv"))
        C.do("scalar", lambda e: e.activation(rq[0:1, 0:T], rq[0:1, 0:T], AF.Sqrt))
        C.do("scalar", lambda e: e.mul(nmqb[0:1, :], rq[0:1, 0:T], -1.0),
             inc=("s_row", 1))
        C.w("tensor", "s_row", C.cnt("s_row"))
        for qg in range(NT):
            qsl = slice(qg * TG, (qg + 1) * TG)
            for kt in range(NKT):
                fns = [
                    (lambda e, ps, first, last, kt=kt, qsl=qsl: e.matmul(
                        ps[:, :], k_h[0:64, kt * 128:(kt + 1) * 128], q_h[0:64, qsl],
                        start=True, stop=False)),
                    (lambda e, ps, first, last, qsl=qsl: e.matmul(
                        ps[:, :], C.ones128b[0:1, :], nmqb[0:1, qsl],
                        start=False, stop=True)),
                ]
                ps = C.rot_tile(fns, mm_dep=hl_tok if kt == 0 and qg == 0 else None)
                C.rot_evac("scalar", [lambda e, ps=ps, kt=kt: e.activation(
                    Pb[:, kt, :], ps[:, :], AF.Exp, scale=0.125)])
            ptok = C.rot_ledger[-1]
            C.aux_guard()
            C.w("tensor", ptok[0], ptok[1])
            for kt in range(NKT):
                C.do("tensor", lambda e, kt=kt: e.matmul(
                    C.psU[0:1, :], C.ones_colb[:, 0:1], Pb[:, kt, :],
                    start=(kt == 0), stop=(kt == NKT - 1)),
                    inc=("s_amm", 1) if kt == NKT - 1 else None)
            C.aux_evac("scalar", [
                (lambda e: e.activation(rk[0:1, 0:TG], C.psU[0:1, :], AF.Ln)),
                (lambda e: e.activation(rk[0:1, 0:TG], rk[0:1, 0:TG],
                                        AF.Exp, scale=-1.0)),
            ])
            C.aux_guard()
            C.do("tensor", lambda e: e.matmul(C.psS[0:64, :], C.ones128[0:1, 0:64],
                                              rk[0:1, 0:TG], start=True, stop=True),
                 inc=("s_amm", 1))
            C.aux_evac("vector", [lambda e: e.tensor_copy(rdb[0:64, :], C.psS[0:64, :])])
            rdb_tok = ("s_aev", C.cnt("s_aev"))
            fns = [(lambda e, ps, first, last, kt=kt: e.matmul(
                ps[0:64, :], v_h[:, kt, :], Pb[:, kt, :], start=first, stop=last))
                for kt in range(NKT)]
            ps = C.rot_tile(fns)
            C.rot_evac("vector", [lambda e, ps=ps, qg=qg, tl=tl, po=po: e.tensor_mul(
                og[po:po + 64, tl, qg * TG:(qg + 1) * TG], ps[0:64, :], rdb[0:64, :])],
                dep=rdb_tok)
    og_done = ("s_rv_vector", C.cnt("s_rv_vector"))
    prev_store = None
    for g in range(NT):
        tx = C.dma("sync", xsl3[:], x_dram[:, g * TG:(g + 1) * TG]
                   .rearrange("(c p) t -> p c t", p=128), semname="s_x3b")
        for do in range(DC):
            fns = [(lambda e, ps, first, last, k=k, do=do, g=g: e.matmul(
                ps[:, :], woT[:, k, do * 128:(do + 1) * 128],
                og[:, k, g * TG:(g + 1) * TG], start=first, stop=last))
                for k in range(DC)]
            ps = C.rot_tile(fns, mm_dep=max_tok(wotok, og_done)
                            if g == 0 and do == 0 else None)
            deps = []
            if do == 0:
                deps = [tx]
                if g == 0:
                    deps.append(tbo)
                if prev_store is not None:
                    deps.append(prev_store)
            C.rot_evac("vector", [lambda e, ps=ps, do=do, g=g: e.scalar_tensor_tensor(
                xsl3[:, do, :], ps[:, :], bocol[:, do:do + 1],
                xsl3[:, do, :], ALU.add, ALU.add)],
                dep=max_tok(*deps) if deps else None)
        C.do("vector", lambda e: e.engine_nop(), inc=("s_xa3b", 1))
        C.w("sync", "s_xa3b", C.cnt("s_xa3b"))
        prev_store = C.dma("sync", xout_dram[:, g * TG:(g + 1) * TG]
                           .rearrange("(c p) t -> p c t", p=128), xsl3[:],
                           semname="s_xs3b")
    C.wtok("sync", prev_store)
    full_barrier_on(C, "sync")



def emit_l2attn(C, x_in_ap, xout_ap, wqkv_ap, bqkvcol_ap, wproj_ap, bpcol_ap,
                maskcol_ap, bvbc_ap, nsb=4):
    TE = 1024
    TSB = 512
    nx = C.sb("a2_nx", [128, DC, TE], BF16)
    og = C.sb("a2_og", [128, DC, TSB], BF16)
    wT = C.sb("a2_wT", [128, DC, 2048], BF16)
    nat = C.sb("a2_nat", [128, 8192], BF16)
    xext = C.sb("a2_xe", [128, DC, TE], F32)
    qf = C.sb("a2_qf", [128, DC, TE], BF16)
    kf = C.sb("a2_kf", [128, DC, TE], BF16)
    xsl = C.sb("a2_xsl", [128, DC, TSB], F32)
    bqkvcol = C.sb("a2_bq", [128, DC, 3], F32)
    bpcol = C.sb("a2_bp", [128, DC], F32)
    maskcol = C.sb("a2_mk", [128, 4, 2, 4], F32)
    bvbc = C.sb("a2_bvb", [128, 1024], F32)
    q_h = C.sb("a2_qh", [64, TSB], BF16)
    k_h = C.sb("a2_kh", [64, TE], BF16)
    v_h = C.sb("a2_vh", [128, TE // 128, 64], BF16)
    Pb = C.sb("a2_P", [128, 4, 256], BF16)
    sq = C.sb("a2_sq", [64, TE], BF16)
    rowb = C.sb("a2_row", [1, 2, TE], F32)
    knm = C.sb("a2_knm", [1, 1], F32)
    nmqb = C.sb("a2_nmq", [1, TSB], BF16)
    rdb = C.sb("a2_rdb", [64, 256], F32)
    tb = C.dma("sync", bqkvcol[:], bqkvcol_ap)
    tp = C.dma("sync", bpcol[:], bpcol_ap)
    tm = C.dma("sync", maskcol[:], maskcol_ap)
    tbv = C.dma("sync", bvbc[:], bvbc_ap)
    prev_store = None
    for sb in range(nsb):
        esl = slice(sb * TSB, sb * TSB + TE)
        C.w("sync", "s_rmm", C.cnt("s_rmm"))
        xtok = C.dma("sync", xext[:], x_in_ap[:, esl].rearrange("(c p) t -> p c t", p=128),
                     semname="s_xld")
        for e in ("tensor", "vector", "scalar", "gpsimd"):
            C.wtok(e, xtok)
        ln_normed(C, xext, TE, double=False, out=nx)
        nx_tok = ("s_aev", C.cnt("s_aev"))
        for pi, obuf in ((0, qf), (1, kf)):
            wtok = load_weight_T(C, wqkv_ap[pi * 1024:(pi + 1) * 1024, :], 1024, 1024,
                                 nat, wT, woff=pi * 1024)
            for g in range(TE // TG):
                for do in range(DC):
                    fns = [(lambda e, ps, first, last, k=k, do=do, g=g, pi=pi: e.matmul(
                        ps[:, :], wT[:, k, pi * 1024 + do * 128:pi * 1024 + (do + 1) * 128],
                        nx[:, k, g * TG:(g + 1) * TG], start=first, stop=last))
                        for k in range(DC)]
                    ps = C.rot_tile(fns, mm_dep=max_tok(nx_tok, wtok)
                                    if g == 0 and do == 0 else None)
                    C.rot_evac("vector", [lambda e, ps=ps, do=do, g=g, obuf=obuf, pi=pi:
                                          e.tensor_scalar(obuf[:, do, g * TG:(g + 1) * TG],
                                                          ps[:, :], bqkvcol[:, do, pi:pi + 1],
                                                          None, ALU.add)],
                               dep=tb if g == 0 and do == 0 and sb == 0 else None)
        wvtok = load_weight_T(C, wqkv_ap[2048:3072, :], 1024, 1024, nat, wT, woff=0)
        for h in range(16):
            tl, po = h // 2, (h % 2) * 64
            C.w("vector", "s_rmm", C.cnt("s_rmm"))
            C.do("vector", lambda e, tl=tl, po=po: e.tensor_copy(
                q_h[0:64, :], qf[po:po + 64, tl, 256:256 + TSB]))
            C.do("vector", lambda e, tl=tl, po=po: e.tensor_copy(
                k_h[0:64, :], kf[po:po + 64, tl, :]), inc=("s_hl", 1))
            hl_tok = ("s_hl", C.cnt("s_hl"))
            for tt in range(TE // 128):
                fns = [(lambda e, ps, first, last, k=k, tt=tt, h=h: e.matmul(
                    ps[:, 0:64], nx[:, k, tt * 128:(tt + 1) * 128],
                    wT[:, k, h * 64:(h + 1) * 64], start=first, stop=last))
                    for k in range(DC)]
                ps = C.rot_tile(fns, TGT=64,
                                mm_dep=wvtok if tt == 0 and h == 0 else None)
                C.rot_evac("vector", [lambda e, ps=ps, tt=tt, h=h: e.tensor_add(
                    v_h[:, tt, :], ps[:, 0:64], bvbc[:, h * 64:(h + 1) * 64])],
                    dep=tbv if tt == 0 and h == 0 and sb == 0 else None)
            v_done = ("s_rv_vector", C.cnt("s_rv_vector"))
            C.w("scalar", "s_aev", C.cnt("s_aev"))
            C.w("scalar", "s_rmm", C.cnt("s_rmm"))
            C.wtok("scalar", hl_tok)
            C.do("scalar", lambda e: e.activation(sq[0:64, :], k_h[0:64, :], AF.Square),
                 inc=("s_sq", 1))
            C.w("tensor", "s_sq", C.cnt("s_sq"))
            for a in range(TE // TG):
                C.aux_guard()
                C.do("tensor", lambda e, a=a: e.matmul(
                    C.psU[0:1, :], C.ones_colb[0:64, 0:1], sq[0:64, a * TG:(a + 1) * TG],
                    start=True, stop=True), inc=("s_amm", 1))
                C.aux_evac("vector", [lambda e, a=a: e.tensor_copy(
                    rowb[0:1, 0, a * TG:(a + 1) * TG], C.psU[0:1, :])])
            C.do("vector", lambda e: e.reduce_max(knm[0:1, 0:1], rowb[0:1, 0, :], AX.X),
                 inc=("s_gv", 1))
            C.w("scalar", "s_gv", C.cnt("s_gv"))
            C.do("scalar", lambda e: e.activation(sq[0:64, 0:TSB], q_h[0:64, :], AF.Square),
                 inc=("s_sq", 1))
            C.w("tensor", "s_sq", C.cnt("s_sq"))
            C.aux_guard()
            C.do("tensor", lambda e: e.matmul(
                C.psU[0:1, :], C.ones_colb[0:64, 0:1], sq[0:64, 0:TSB],
                start=True, stop=True), inc=("s_amm", 1))
            C.aux_evac("vector", [lambda e: e.tensor_copy(
                rowb[0:1, 1, 0:TSB], C.psU[0:1, :])])
            C.do("vector", lambda e: e.tensor_scalar(rowb[0:1, 1, 0:TSB], rowb[0:1, 1, 0:TSB],
                                                     knm[0:1, 0:1], None, ALU.mult),
                 inc=("s_gv", 1))
            C.w("scalar", "s_gv", C.cnt("s_gv"))
            C.do("scalar", lambda e: e.activation(rowb[0:1, 1, 0:TSB], rowb[0:1, 1, 0:TSB],
                                                  AF.Sqrt))
            C.do("scalar", lambda e: e.mul(nmqb[0:1, :], rowb[0:1, 1, 0:TSB], -1.0),
                 inc=("s_row", 1))
            C.w("tensor", "s_row", C.cnt("s_row"))
            for g in range(2):
                qsl = slice(g * 256, (g + 1) * 256)
                kbase = 128 + g * 256
                for kt in range(4):
                    fns = [
                        (lambda e, ps, first, last, kt=kt, qsl=qsl, kbase=kbase: e.matmul(
                            ps[:, 0:256], k_h[0:64, kbase + kt * 128:kbase + (kt + 1) * 128],
                            q_h[0:64, qsl], start=True, stop=False)),
                        (lambda e, ps, first, last, qsl=qsl: e.matmul(
                            ps[:, 0:256], C.ones128b[0:1, :], nmqb[0:1, qsl],
                            start=False, stop=True)),
                    ]
                    ps = C.rot_tile(fns, TGT=256,
                                    mm_dep=hl_tok if kt == 0 and g == 0 else None)
                    C.rot_evac("scalar", [lambda e, ps=ps, kt=kt, g=g, sb=sb: e.activation(
                        Pb[:, kt, :], ps[:, 0:256], AF.Exp, scale=0.125,
                        bias=maskcol[:, sb, g, kt:kt + 1])],
                        dep=tm if kt == 0 and g == 0 and h == 0 and sb == 0 else None)
                ptok = C.rot_ledger[-1]
                C.aux_guard()
                C.w("tensor", ptok[0], ptok[1])
                for kt in range(4):
                    C.do("tensor", lambda e, kt=kt: e.matmul(
                        C.psU[0:1, 0:256], C.ones_colb[:, 0:1], Pb[:, kt, :],
                        start=(kt == 0), stop=(kt == 3)),
                        inc=("s_amm", 1) if kt == 3 else None)
                C.aux_evac("scalar", [
                    (lambda e: e.activation(rowb[0:1, 0, 0:256], C.psU[0:1, 0:256], AF.Ln)),
                    (lambda e: e.activation(rowb[0:1, 0, 0:256], rowb[0:1, 0, 0:256],
                                            AF.Exp, scale=-1.0)),
                ])
                C.aux_guard()
                C.do("tensor", lambda e: e.matmul(C.psS[0:64, 0:256], C.ones128[0:1, 0:64],
                                                  rowb[0:1, 0, 0:256], start=True, stop=True),
                     inc=("s_amm", 1))
                C.aux_evac("vector", [lambda e: e.tensor_copy(rdb[0:64, :],
                                                              C.psS[0:64, 0:256])])
                rdb_tok = ("s_aev", C.cnt("s_aev"))
                ktile0 = kbase // 128
                fns = [(lambda e, ps, first, last, kt=kt, ktile0=ktile0: e.matmul(
                    ps[0:64, 0:256], v_h[:, ktile0 + kt, :], Pb[:, kt, :],
                    start=first, stop=last)) for kt in range(4)]
                ps = C.rot_tile(fns, TGT=256, mm_dep=v_done if g == 0 else None)
                C.rot_evac("vector", [lambda e, ps=ps, g=g, tl=tl, po=po: e.tensor_mul(
                    og[po:po + 64, tl, g * 256:(g + 1) * 256], ps[0:64, 0:256],
                    rdb[0:64, :])], dep=rdb_tok)
        wptok = load_weight_T(C, wproj_ap, 1024, 1024, nat, wT, woff=0)
        og_done = ("s_rv_vector", C.cnt("s_rv_vector"))
        tx = C.dma("sync", xsl[:], x_in_ap[:, 256 + sb * TSB:256 + (sb + 1) * TSB]
                   .rearrange("(c p) t -> p c t", p=128), semname="s_x2")
        for do in range(DC):
            fns = [(lambda e, ps, first, last, k=k, do=do: e.matmul(
                ps[:, :], wT[:, k, do * 128:(do + 1) * 128],
                og[:, k, :], start=first, stop=last))
                for k in range(DC)]
            ps = C.rot_tile(fns, mm_dep=max_tok(wptok, og_done) if do == 0 else None)
            deps = []
            if do == 0:
                deps = [tx]
                if sb == 0:
                    deps.append(tp)
                if prev_store is not None:
                    deps.append(prev_store)
            C.rot_evac("vector", [lambda e, ps=ps, do=do: e.scalar_tensor_tensor(
                xsl[:, do, :], ps[:, :], bpcol[:, do:do + 1], xsl[:, do, :],
                ALU.add, ALU.add)], dep=max_tok(*deps) if deps else None)
        C.do("vector", lambda e: e.engine_nop(), inc=("s_xa2", 1))
        C.w("sync", "s_xa2", C.cnt("s_xa2"))
        prev_store = C.dma("sync", xout_ap[:, sb * TSB:(sb + 1) * TSB]
                           .rearrange("(c p) t -> p c t", p=128), xsl[:],
                           semname="s_xs2")
    C.wtok("sync", prev_store)


GOLDEN_CENTER = 0.36787944117144233
GOLDEN_LOWER = 0.5 - 0.28768207245178085
GOLDEN_UPPER = 0.5


def emit_l3ffn(C, x_dram, xout_dram, T, wr_ap, wg_ap, wu_ap, wd_ap, temp_ap, NE=8):
    NTT = T // 128
    nx = C.sb("t3_nx", [128, DC, T], BF16)
    wgb = C.sb("t3_wg", [128, DC, 1024], BF16)
    wub = C.sb("t3_wu", [128, DC, 1024], BF16)
    wdb = C.sb("t3_wd", [128, DC, 1024], BF16)
    wrb = C.sb("t3_wr", [128, DC, 8], F32)
    cw = C.sb("t3_cw", [128, NTT, 8], F32)
    lg = C.sb("t3_lg", [128, NTT, 8], F32)
    gsc = C.sb("t3_gsc", [128, 40], F32)
    ones8 = C.sb("t3_on8", [128, 8], F32)
    msk8 = C.sb("t3_m8", [128, 8], U8)
    Rcol = C.sb("t3_Rcol", [128, NTT], F32)
    csum = C.sb("t3_csum", [1, 8], F32)
    rtc = C.sb("t3_rtc", [128, 1], F32)
    tmp1 = C.sb("t3_tmp1", [1, 1], F32)
    acc_d = C.dram("t3_accd", [T, 1024], F32)
    C.do("gpsimd", lambda e: e.memset(ones8[:], 1.0), inc=("s_pr", 1))
    C.make_arena(64)
    xbuf = C.carve(0, [128, DC, T], F32)
    xtok = C.dma("sync", xbuf, x_dram.rearrange("(c p) t -> p c t", p=128))
    for e in ("tensor", "vector", "scalar", "gpsimd"):
        C.wtok(e, xtok)
    row = ln_normed(C, xbuf, T, double=False, out=nx)
    nx_tok = ("s_aev", C.cnt("s_aev"))
    Mcol3 = C.sb("t3_Mcol", [128, NTT], F32)
    C.w("tensor", "s_row", C.cnt("s_row"))
    C.aux_guard()
    for i in range(NTT):
        C.do("tensor", lambda e, i=i: e.matmul(
            C.psU[:, i:i + 1], row[0:1, 2, i * 128:(i + 1) * 128],
            C.ones_col[0:1, 0:1], start=True, stop=True))
        C.do("tensor", lambda e, i=i: e.matmul(
            C.psU[:, NTT + i:NTT + i + 1], row[0:1, 1, i * 128:(i + 1) * 128],
            C.ones_col[0:1, 0:1], start=True, stop=True),
            inc=("s_amm", 1) if i == NTT - 1 else None)
    C.aux_evac("vector", [
        (lambda e: e.tensor_copy(Rcol[:], C.psU[:, 0:NTT])),
        (lambda e: e.tensor_copy(Mcol3[:], C.psU[:, NTT:2 * NTT])),
    ])
    tmp2 = C.sb("t3_tmp2", [1, 1], F32)
    t_tmp = C.dma("sync", tmp1[:], temp_ap)
    C.wtok("vector", t_tmp)
    C.do("vector", lambda e: e.reciprocal(tmp2[:], tmp1[:]), inc=("s_x", 1))
    C.w("tensor", "s_x", C.cnt("s_x"))
    C.aux_guard()
    C.do("tensor", lambda e: e.matmul(C.psS[:, 0:1], C.ones128[0:1, :], tmp2[0:1, 0:1],
                                      start=True, stop=True), inc=("s_amm", 1))
    C.aux_evac("vector", [lambda e: e.tensor_copy(rtc[:], C.psS[:, 0:1])])
    t_wr = C.dma("sync", wrb[:], wr_ap.rearrange("(a p) o -> p a o", p=128))
    C.wtok("tensor", t_wr)
    C.aux_guard()
    for k in range(DC):
        C.do("tensor", lambda e, k=k: e.matmul(
            C.psU[0:1, 0:8], C.ones_col[:, 0:1], wrb[:, k, :],
            start=(k == 0), stop=(k == DC - 1)), inc=("s_amm", 1) if k == DC - 1 else None)
    C.aux_evac("vector", [lambda e: e.tensor_scalar(csum[0:1, :], C.psU[0:1, 0:8],
                                                    -1.0, None, ALU.mult)])
    csbc3 = C.sb("t3_csbc", [128, 8], F32)
    C.w("tensor", "s_aev", C.cnt("s_aev"))
    C.aux_guard()
    C.do("tensor", lambda e: e.matmul(C.psU[:, 0:8], C.ones128[0:1, :], csum[0:1, :],
                                      start=True, stop=True), inc=("s_amm", 1))
    C.aux_evac("vector", [lambda e: e.tensor_copy(csbc3[:], C.psU[:, 0:8])])
    csum_tok = ("s_aev", C.cnt("s_aev"))
    lgx = C.carve(48.0, [128, NTT * 8], F32)
    B1 = C.carve(48.5, [128, NTT * 8], F32)
    B2 = C.carve(49.0, [128, NTT * 8], F32)
    B3 = C.carve(49.5, [128, NTT * 8], F32)
    B4 = C.carve(50.0, [128, NTT * 8], F32)
    B5 = C.carve(50.5, [128, NTT * 8], F32)
    B6 = C.carve(51.0, [128, NTT * 8], F32)
    mskx = C.carve(51.5, [128, NTT * 8], U8)
    c8 = C.carve(51.75, [128, 10, NTT], F32)
    TS8 = [slice(i * 8, (i + 1) * 8) for i in range(NTT)]
    C.w("tensor", "s_row", C.cnt("s_row"))
    C.w("tensor", "s_aev", C.cnt("s_aev"))
    for i in range(NTT):
        for k in range(DC):
            C.do("tensor", lambda e, k=k, i=i: e.matmul(
                C.psR[:, i * 8:(i + 1) * 8], xbuf[:, k, i * 128:(i + 1) * 128],
                wrb[:, k, :], start=(k == 0), stop=(k == DC - 1)),
                inc=("s_rmm", 1) if k == DC - 1 else None)
    C.w("vector", "s_rmm", C.cnt("s_rmm"))
    C.w("vector", "s_pr", C.cnt("s_pr"))
    C.do("vector", lambda e: e.tensor_copy(lgx, C.psR[:, 0:NTT * 8]),
         inc=("s_lg3", 1))
    lgev_tok = ("s_lg3", C.cnt("s_lg3"))
    for i, sl in enumerate(TS8):
        C.do("vector", lambda e, sl=sl, i=i: e.tensor_scalar(
            B1[:, sl], csbc3[:], Mcol3[:, i:i + 1], None, ALU.mult))
        C.do("vector", lambda e, sl=sl, i=i: e.tensor_scalar(
            B2[:, sl], ones8[:], Rcol[:, i:i + 1], rtc[:, 0:1], ALU.mult, ALU.mult))
    C.do("vector", lambda e: e.tensor_add(lgx, lgx, B1))
    C.do("vector", lambda e: e.tensor_mul(lgx, lgx, B2), inc=("s_gv", 1))
    C.w("scalar", "s_gv", C.cnt("s_gv"))
    for i, sl in enumerate(TS8):
        C.do("scalar", lambda e, sl=sl: e.activation(B1[:, sl], lgx[:, sl], AF.Sigmoid),
             inc=("s_gs", 1) if i == NTT - 1 else None)
    C.w("vector", "s_gs", C.cnt("s_gs"))
    for i, sl in enumerate(TS8):
        C.do("vector", lambda e, sl=sl: e.tensor_scalar(
            B2[:, sl], B1[:, sl], GOLDEN_CENTER, None, ALU.subtract))
    for i, sl in enumerate(TS8):
        C.do("vector", lambda e, sl=sl: e.tensor_scalar(
            B3[:, sl], B2[:, sl], -1.0, None, ALU.mult))
    for i, sl in enumerate(TS8):
        C.do("vector", lambda e, sl=sl: e.tensor_max(B4[:, sl], B2[:, sl], B3[:, sl]))
    for i, sl in enumerate(TS8):
        C.do("vector", lambda e, sl=sl: e.tensor_scalar(
            B5[:, sl], B1[:, sl], GOLDEN_LOWER, None, ALU.is_ge))
    for i, sl in enumerate(TS8):
        C.do("vector", lambda e, sl=sl: e.tensor_scalar(
            B6[:, sl], B1[:, sl], GOLDEN_UPPER, None, ALU.is_le))
    for i, sl in enumerate(TS8):
        C.do("vector", lambda e, sl=sl: e.tensor_mul(B2[:, sl], B5[:, sl], B6[:, sl]),
             inc=("s_gv", 1) if i == NTT - 1 else None)
    C.w("scalar", "s_gv", C.cnt("s_gv"))
    for i, sl in enumerate(TS8):
        C.do("scalar", lambda e, sl=sl: e.activation(B3[:, sl], B4[:, sl], AF.Exp,
                                                     scale=-10.0))
    for i, sl in enumerate(TS8):
        C.do("scalar", lambda e, sl=sl: e.activation(B5[:, sl], B4[:, sl], AF.Exp,
                                                     scale=-10.0 / 3.0),
             inc=("s_gs", 1) if i == NTT - 1 else None)
    C.w("vector", "s_gs", C.cnt("s_gs"))
    for i, sl in enumerate(TS8):
        C.do("vector", lambda e, sl=sl: e.tensor_mul(B6[:, sl], B3[:, sl], B2[:, sl]))
    for i, sl in enumerate(TS8):
        C.do("vector", lambda e, sl=sl, i=i: e.reduce_sum(c8[:, 0, i:i + 1], B6[:, sl], AX.X))
    for i, sl in enumerate(TS8):
        C.do("vector", lambda e, sl=sl, i=i: e.reduce_max(c8[:, 1, i:i + 1], B5[:, sl], AX.X))
    for i, sl in enumerate(TS8):
        C.do("vector", lambda e, sl=sl, i=i: e.tensor_scalar(
            B1[:, sl], B5[:, sl], c8[:, 1, i:i + 1], None, ALU.is_equal))
    for i, sl in enumerate(TS8):
        C.do("vector", lambda e, sl=sl: e.scalar_tensor_tensor(
            B4[:, sl], B1[:, sl], -1e30, B5[:, sl], ALU.mult, ALU.add))
    for i, sl in enumerate(TS8):
        C.do("vector", lambda e, sl=sl, i=i: e.reduce_max(c8[:, 2, i:i + 1], B4[:, sl], AX.X))
    for i, sl in enumerate(TS8):
        C.do("vector", lambda e, sl=sl, i=i: e.tensor_scalar(
            B2[:, sl], B5[:, sl], c8[:, 2, i:i + 1], None, ALU.is_ge))
    for i, sl in enumerate(TS8):
        C.do("vector", lambda e, sl=sl: e.tensor_mul(B3[:, sl], B5[:, sl], B2[:, sl]))
    for i, sl in enumerate(TS8):
        C.do("vector", lambda e, sl=sl, i=i: e.reduce_sum(c8[:, 3, i:i + 1], B3[:, sl], AX.X))
    for i in range(NTT):
        C.do("vector", lambda e, i=i: e.tensor_scalar(
            c8[:, 4, i:i + 1], c8[:, 3, i:i + 1], 1e-8, None, ALU.max))
    for i in range(NTT):
        C.do("vector", lambda e, i=i: e.reciprocal(c8[:, 5, i:i + 1], c8[:, 4, i:i + 1]))
    for i, sl in enumerate(TS8):
        C.do("vector", lambda e, sl=sl, i=i: e.tensor_scalar(
            B4[:, sl], B3[:, sl], c8[:, 5, i:i + 1], None, ALU.mult))
    for i in range(NTT):
        C.do("vector", lambda e, i=i: e.tensor_scalar(
            c8[:, 6, i:i + 1], c8[:, 0, i:i + 1], 1e-8, None, ALU.is_lt))
    for i, sl in enumerate(TS8):
        C.do("vector", lambda e, sl=sl, i=i: e.tensor_scalar(
            mskx[:, sl], ones8[:], c8[:, 6, i:i + 1], None, ALU.mult))
    for i, sl in enumerate(TS8):
        C.do("vector", lambda e, sl=sl: e.select(B5[:, sl], mskx[:, sl], B4[:, sl], B6[:, sl]))
    for i, sl in enumerate(TS8):
        C.do("vector", lambda e, sl=sl, i=i: e.reduce_sum(c8[:, 7, i:i + 1], B5[:, sl], AX.X))
    for i in range(NTT):
        C.do("vector", lambda e, i=i: e.tensor_scalar(
            c8[:, 8, i:i + 1], c8[:, 7, i:i + 1], 1e-8, None, ALU.max))
    for i in range(NTT):
        C.do("vector", lambda e, i=i: e.reciprocal(c8[:, 9, i:i + 1], c8[:, 8, i:i + 1]))
    for i, sl in enumerate(TS8):
        C.do("vector", lambda e, sl=sl, i=i: e.tensor_scalar(
            cw[:, i, :], B5[:, sl], c8[:, 9, i:i + 1], None, ALU.mult),
            inc=("s_cw", 1) if i == NTT - 1 else None)
    cw_tok = ("s_cw", C.cnt("s_cw"))
    gbuf = C.carve(0, [128, DC, T], BF16)
    outsb = C.carve(32, [128, 4, 1024], F32)
    C.sem("s_fill3")
    slot_uses = [0, 0, 0, 0]
    for e_i in range(NE):
        C.w("gpsimd", "s_rmm", C.cnt("s_rmm"))
        if e_i == 0:
            C.w("gpsimd", "s_cw", C.cnt("s_cw"))
            C.w("gpsimd", "s_aev", C.cnt("s_aev"))
        t_wg = C.dma("gpsimd", wgb[:], wg_ap[e_i].rearrange("(a p) o -> p a o", p=128),
                     semname="s_wl3")
        t_wu = C.dma("gpsimd", wub[:], wu_ap[e_i].rearrange("(a p) o -> p a o", p=128),
                     semname="s_wl3")
        t_wd = C.dma("gpsimd", wdb[:], wd_ap[e_i].rearrange("(a p) o -> p a o", p=128),
                     semname="s_wl3")
        wtok = ("s_wl3", C.cnt("s_wl3"))
        silu_base = C.cnt("s_rv_scalar")
        for g in range(T // TG):
            for do in range(DC):
                fns = [(lambda e, ps, first, last, k=k, do=do, g=g: e.matmul(
                    ps[:, :], wgb[:, k, do * 128:(do + 1) * 128],
                    nx[:, k, g * TG:(g + 1) * TG], start=first, stop=last))
                    for k in range(DC)]
                ps = C.rot_tile(fns, mm_dep=max_tok(nx_tok, wtok, lgev_tok)
                                if g == 0 and do == 0 else None)
                C.rot_evac("scalar", [lambda e, ps=ps, do=do, g=g: e.activation(
                    gbuf[:, do, g * TG:(g + 1) * TG], ps[:, :], AF.Silu)])
        for g in range(T // TG):
            for do in range(DC):
                fns = [(lambda e, ps, first, last, k=k, do=do, g=g: e.matmul(
                    ps[:, :], wub[:, k, do * 128:(do + 1) * 128],
                    nx[:, k, g * TG:(g + 1) * TG], start=first, stop=last))
                    for k in range(DC)]
                ps = C.rot_tile(fns)
                C.rot_evac("vector", [lambda e, ps=ps, do=do, g=g: e.tensor_mul(
                    gbuf[:, do, g * TG:(g + 1) * TG], ps[:, :],
                    gbuf[:, do, g * TG:(g + 1) * TG])],
                    dep=("s_rv_scalar", silu_base + g * DC + do + 1))
        up_done = ("s_rv_vector", C.cnt("s_rv_vector"))
        for tt in range(NTT):
            slot = tt % 4
            for dh in range(2):
                fns = [(lambda e, ps, first, last, k=k, tt=tt, dh=dh: e.matmul(
                    ps[:, :], gbuf[:, k, tt * 128:(tt + 1) * 128],
                    wdb[:, k, dh * 512:(dh + 1) * 512], start=first, stop=last))
                    for k in range(DC)]
                ps = C.rot_tile(fns, mm_dep=up_done if tt == 0 and dh == 0 else None)
                deps = []
                if dh == 0 and slot_uses[slot]:
                    deps.append(("s_ac3%d" % slot, 16 * slot_uses[slot]))
                C.rot_evac("vector", [
                    lambda e, ps=ps, slot=slot, dh=dh, tt=tt, e_i=e_i: e.tensor_scalar(
                        outsb[:, slot, dh * 512:(dh + 1) * 512], ps[:, :],
                        cw[:, tt, e_i:e_i + 1], None, ALU.mult)],
                    dep=max_tok(*deps) if deps else None)
            C.do("vector", lambda e: e.engine_nop(), inc=("s_fill3", 1))
            C.w("gpsimd", "s_fill3", C.cnt("s_fill3"))
            C.dma("gpsimd", acc_d[tt * 128:(tt + 1) * 128, :], outsb[:, slot, :],
                  semname="s_ac3%d" % slot,
                  accum_op=ALU.bypass if e_i == 0 else ALU.add)
            slot_uses[slot] += 1
    for sl in range(4):
        C.w("sync", "s_ac3%d" % sl, 16 * slot_uses[sl])
    C.w("sync", "s_rmm", C.cnt("s_rmm"))
    accq = C.carve(40, [128, 4, 1024], F32)
    xsl = C.carve(56, [128, DC, 128], F32)
    ybf = C.carve(60, [128, 1024], BF16)
    prev_store = None
    for i in range(NTT):
        qi = i % 4
        if qi == 0:
            C.w("sync", "s_yb", C.cnt("s_yb"))
            tacc = C.dma("sync", accq, acc_d[i * 128:(i + 4) * 128, :]
                         .rearrange("(a p) d -> p a d", p=128), semname="s_al")
            C.wtok("vector", tacc)
        C.do("vector", lambda e, i=i, qi=qi: e.tensor_copy(ybf, accq[:, qi, :]),
             inc=("s_yb", 1))
        t_x = C.dma("sync", xsl, x_dram[:, i * 128:(i + 1) * 128]
                    .rearrange("(c p) t -> p c t", p=128), semname="s_xl3")
        C.w("tensor", "s_yb", C.cnt("s_yb"))
        for c in range(DC):
            ps_ap, tpv = pe_transpose(C, ybf[:, c * 128:(c + 1) * 128])
            if c == 0:
                C.wtok("vector", t_x)
                if prev_store is not None:
                    C.wtok("vector", prev_store)
            tp_consume(C, "vector", [
                lambda e, ps_ap=ps_ap, c=c: e.scalar_tensor_tensor(
                    xsl[:, c, :], ps_ap, 1.0, xsl[:, c, :], ALU.mult, ALU.add)], tpv)
        C.do("vector", lambda e: e.engine_nop(), inc=("s_xa3", 1))
        C.w("sync", "s_xa3", C.cnt("s_xa3"))
        prev_store = C.dma("sync", xout_dram[:, i * 128:(i + 1) * 128]
                           .rearrange("(c p) t -> p c t", p=128), xsl, semname="s_xs3")
    C.wtok("sync", prev_store)



def emit_l1attn(C, x_dram, xout_dram, wk_ap, wv_ap, wr_ap, wo_ap, lng_ap, lnb_ap,
                tmk_ap, tmv_ap, tmr_ap, tdlog_ap, nxp_ap, carry_ap,
                carryout_ap=None, nxlast_ap=None, T=2048, dbg_stop=None):
    TSB = 512
    NSB = T // TSB
    NTGS = TSB // TG
    nx = C.sb("w_nx", [128, DC, TSB], BF16)
    ekb = C.sb("w_ek", [128, DC, TSB], BF16)
    C._l1nx = nx; C._l1ek = ekb
    C._l1v = C.sb("w_v", [128, DC, TSB], BF16)
    C._l1r = C.sb("w_r", [128, DC, TSB], BF16)
    C.t_ln2 = C.sb("w_tln2", [128, TSB], BF16)
    wT = C.sb("w_wT", [128, DC, 1024], BF16)
    nat = C.sb("w_nat", [128, 8192], BF16)
    lng = C.sb("w_lng", [1, DC, 128], F32)
    lnb = C.sb("w_lnb", [1, DC, 128], F32)
    tmk3 = [C.sb(f"w_tm{i}", [128, DC], F32) for i in range(3)]
    tdc = C.sb("w_td", [128, DC], F32)
    nxp = C.sb("w_nxp", [128, DC, 1], BF16)
    nxprev = C.sb("w_nxp2", [128, DC, 1], BF16)
    carry = C.sb("w_carry", [128, DC, 2], F32)
    onesb = C.sb("w_ones", [128, TSB], BF16)
    sgt = C.sb("w_sg", [128, TG], BF16)
    t1 = C.dma("sync", lng[:], lng_ap)
    t2 = C.dma("sync", lnb[:], lnb_ap)
    t3 = C.dma("sync", tmk3[0][:], tmk_ap)
    t4 = C.dma("sync", tmk3[1][:], tmv_ap)
    t5 = C.dma("sync", tmk3[2][:], tmr_ap)
    t6 = C.dma("sync", tdc[:], tdlog_ap)
    t7 = C.dma("gpsimd", nxp[:], nxp_ap)
    t8 = C.dma("sync", carry[:], carry_ap)
    C.do("gpsimd", lambda e: e.memset(onesb[:], 1.0), inc=("s_pr", 1))
    C.wtok("scalar", t6)
    C.do("scalar", lambda e: e.activation(tdc[:], tdc[:], AF.Exp), inc=("s_td", 1))
    xbuf = C.sb("w_xbuf", [128, DC, TSB], F32)
    mixb = C.sb("w_mix", [128, DC, TSB], BF16)
    scr = [C.sb(f"w_scr{j}", [128, TSB], F32) for j in range(3)]
    tdb = C.sb("w_tdb", [128, TSB], F32)
    xslw = C.sb("w_xsl", [128, DC, TG], F32)
    for sb in range(NSB):
        tsl = slice(sb * TSB, (sb + 1) * TSB)
        C.w("sync", "s_rmm", C.cnt("s_rmm"))
        C.w("sync", "s_wkv", C.cnt("s_wkv"))
        C.w("sync", "s_rw", C.cnt("s_rw"))
        xtok = C.dma("sync", xbuf[:], x_dram[:, tsl].rearrange("(c p) t -> p c t", p=128),
                     semname="s_xw")
        for e in ("tensor", "vector", "scalar", "gpsimd"):
            C.wtok(e, xtok)
        if sb == 0:
            C.wtok("tensor", t1)
            C.wtok("tensor", t2)
        ln_normed(C, xbuf, TSB, double=True, lnw_row=lng, lnb_row=lnb, out=nx)
        nx_tok = ("s_aev", C.cnt("s_aev"))
        if dbg_stop == "ln":
            return
        prev_tok = nxprev if sb > 0 else nxp
        wks = ((wk_ap, 0), (wv_ap, 1), (wr_ap, 2))
        for w_ap, mi in wks:
            C.w("vector", "s_rmm", C.cnt("s_rmm"))
            C.w("vector", "s_aev", C.cnt("s_aev"))
            if mi == 0 and sb == 0:
                C.wtok("vector", t3)
                C.wtok("vector", t4)
                C.wtok("vector", t5)
                C.wtok("vector", t7)
            for c in range(DC):
                C.do("vector", lambda e, c=c, mi=mi: e.tensor_sub(
                    C.t_ln2[:, 0:TSB - 1], nx[:, c, 1:TSB], nx[:, c, 0:TSB - 1]))
                C.do("vector", lambda e, c=c, mi=mi: e.scalar_tensor_tensor(
                    mixb[:, c, 1:TSB], C.t_ln2[:, 0:TSB - 1], tmk3[mi][:, c:c + 1],
                    nx[:, c, 0:TSB - 1], ALU.mult, ALU.add))
            for c in range(DC):
                C.do("vector", lambda e, c=c, prev_tok=prev_tok: e.tensor_sub(
                    tdb[:, c:c + 1], nx[:, c, 0:1], prev_tok[:, c, 0:1]))
            for c in range(DC):
                C.do("vector", lambda e, c=c, mi=mi, prev_tok=prev_tok:
                     e.scalar_tensor_tensor(
                         mixb[:, c, 0:1], tdb[:, c:c + 1], tmk3[mi][:, c:c + 1],
                         prev_tok[:, c, 0:1], ALU.mult, ALU.add),
                     inc=("s_mx", 1) if c == DC - 1 else None)
            mix_tok = ("s_mx", C.cnt("s_mx"))
            wtok = load_weight_T(C, w_ap, 1024, 1024, nat, wT)
            for c in range(DC):
                for g in range(NTGS):
                    fns = [(lambda e, ps, first, last, k=k, c=c, g=g: e.matmul(
                        ps[:, :], wT[:, k, c * 128:(c + 1) * 128],
                        mixb[:, k, g * TG:(g + 1) * TG], start=first, stop=last))
                        for k in range(DC)]
                    ps = C.rot_tile(fns, mm_dep=max_tok(mix_tok, wtok)
                                    if c == 0 and g == 0 else None)
                    if mi == 0:
                        C.rot_evac("scalar", [lambda e, ps=ps, c=c, g=g: e.activation(
                            ekb[:, c, g * TG:(g + 1) * TG], ps[:, :], AF.Exp)])
                    elif mi == 1:
                        C.rot_evac("vector", [lambda e, ps=ps, c=c, g=g: e.tensor_scalar(
                            ekb[:, c, g * TG:(g + 1) * TG] if False else
                            C._l1v[:, c, g * TG:(g + 1) * TG],
                            ps[:, :], 1.0, None, ALU.mult)])
                    else:
                        C.rot_evac("scalar", [lambda e, ps=ps, c=c, g=g: e.activation(
                            C._l1r[:, c, g * TG:(g + 1) * TG], ps[:, :], AF.Sigmoid)])
            if mi == 0:
                pass
            if mi == 1:
                vdone = ("s_rv_vector", C.cnt("s_rv_vector"))
                ekdone = ("s_rv_scalar", C.cnt("s_rv_scalar"))
                C.w("vector", "s_td", C.cnt("s_td"))
                C.w("vector", ekdone[0], ekdone[1])
                C.w("vector", vdone[0], vdone[1])
                if sb == 0:
                    C.wtok("vector", t8)
                for c in range(DC):
                    ek_f, v_f, den_f = scr
                    ek_f, v_f, den_f = ek_f[:], v_f[:], den_f[:]
                    C.do("vector", lambda e, c=c, ek_f=ek_f: e.tensor_scalar(
                        tdb[:], onesb[:], tdc[:, c:c + 1], None, ALU.mult))
                    C.do("vector", lambda e, c=c, ek_f=ek_f: e.tensor_copy(ek_f, ekb[:, c, :]))
                    C.do("vector", lambda e, c=c, v_f=v_f: e.tensor_copy(v_f, C._l1v[:, c, :]))
                    C.do("vector", lambda e, c=c, den_f=den_f, ek_f=ek_f: e.tensor_tensor_scan(
                        den_f, tdb[:], ek_f, carry[:, c, 1:2], ALU.mult, ALU.add))
                    C.do("vector", lambda e, c=c, ek_f=ek_f, v_f=v_f: e.tensor_mul(ek_f, ek_f, v_f))
                    C.do("vector", lambda e, c=c, v_f=v_f, ek_f=ek_f: e.tensor_tensor_scan(
                        v_f, tdb[:], ek_f, carry[:, c, 0:1], ALU.mult, ALU.add))
                    C.do("vector", lambda e, c=c, den_f=den_f, ek_f=ek_f: e.tensor_scalar(
                        ek_f, den_f, 1e-8, None, ALU.add), inc=("s_gv", 1))
                    C.do("vector", lambda e, c=c, v_f=v_f: e.tensor_copy(
                        carry[:, c, 0:1], v_f[:, TSB - 1:TSB]))
                    C.do("vector", lambda e, c=c, den_f=den_f: e.tensor_copy(
                        carry[:, c, 1:2], den_f[:, TSB - 1:TSB]))
                    C.w("scalar", "s_gv", C.cnt("s_gv"))
                    C.do("scalar", lambda e, c=c, ek_f=ek_f: e.activation(ek_f, ek_f, AF.Ln))
                    C.do("scalar", lambda e, c=c, ek_f=ek_f: e.activation(ek_f, ek_f, AF.Exp,
                                                                          scale=-1.0),
                         inc=("s_gs", 1))
                    C.w("vector", "s_gs", C.cnt("s_gs"))
                    C.do("vector", lambda e, c=c, v_f=v_f, ek_f=ek_f: e.tensor_mul(v_f, v_f, ek_f))
                    C.do("vector", lambda e, c=c, v_f=v_f: e.tensor_copy(ekb[:, c, :], v_f))
                C.do("vector", lambda e: e.engine_nop(), inc=("s_wkv", 1))
        rdone = ("s_rv_scalar", C.cnt("s_rv_scalar"))
        C.w("vector", rdone[0], rdone[1])
        C.w("vector", "s_wkv", C.cnt("s_wkv"))
        for c in range(DC):
            C.do("vector", lambda e, c=c: e.tensor_mul(ekb[:, c, :], ekb[:, c, :],
                                                       C._l1r[:, c, :]))
        C.do("vector", lambda e: e.engine_nop(), inc=("s_rw", 1))
        rw_tok = ("s_rw", C.cnt("s_rw"))
        C.do("vector", lambda e: e.tensor_copy(nxprev[:], nx[:, :, TSB - 1:TSB]))
        if sb == NSB - 1 and nxlast_ap is not None:
            C.do("vector", lambda e: e.engine_nop(), inc=("s_nxl", 1))
        wotok = load_weight_T(C, wo_ap, 1024, 1024, nat, wT)
        xsl = xslw
        prev_store = None
        for g in range(NTGS):
            tx = C.dma("sync", xsl[:],
                       x_dram[:, sb * TSB + g * TG:sb * TSB + (g + 1) * TG]
                       .rearrange("(c p) t -> p c t", p=128), semname="s_x2w")
            for do in range(DC):
                fns = [(lambda e, ps, first, last, k=k, do=do, g=g: e.matmul(
                    ps[:, :], wT[:, k, do * 128:(do + 1) * 128],
                    ekb[:, k, g * TG:(g + 1) * TG], start=first, stop=last))
                    for k in range(DC)]
                ps = C.rot_tile(fns, mm_dep=max_tok(wotok, rw_tok)
                                if g == 0 and do == 0 else None)
                deps = []
                if do == 0:
                    deps = [tx]
                    if prev_store is not None:
                        deps.append(prev_store)
                C.rot_evac("vector", [lambda e, ps=ps, do=do: e.scalar_tensor_tensor(
                    xsl[:, do, :], ps[:, :], 1.0, xsl[:, do, :], ALU.mult, ALU.add)],
                    dep=max_tok(*deps) if deps else None)
            C.do("vector", lambda e: e.engine_nop(), inc=("s_xaw", 1))
            C.w("sync", "s_xaw", C.cnt("s_xaw"))
            prev_store = C.dma("sync",
                               xout_dram[:, sb * TSB + g * TG:sb * TSB + (g + 1) * TG]
                               .rearrange("(c p) t -> p c t", p=128), xsl[:],
                               semname="s_xsw")
    C.wtok("sync", prev_store)
    if carryout_ap is not None:
        C.w("sync", "s_wkv", C.cnt("s_wkv"))
        tco = C.dma("sync", carryout_ap, carry[:])
        C.wtok("sync", tco)
    if nxlast_ap is not None:
        C.w("sync", "s_nxl", C.cnt("s_nxl"))
        tnl = C.dma("sync", nxlast_ap, nxprev[:])
        C.wtok("sync", tnl)


GAMMAS = [1.0 - 2.0 ** (-5.0 - h) for h in range(4)]


def emit_l0attn(C, x_dram, xout_dram, wq_ap, wk_ap, wv_ap, wg_ap, wo_ap,
                lnw_ap, lnb_ap, gnw_ap, cos_ap, sin_ap, dmt_ap, cdec_ap, kdec_ap,
                sin_state_ap, sout_ap=None, T=2048, state_only=False):
    TSB = 512
    NSB = T // TSB
    NCH = TSB // 128
    CH = 128
    xbuf = C.sb("r_xbuf", [128, DC, TSB], F32)
    nx = C.sb("r_nx", [128, DC, TSB], BF16)
    lnw = C.sb("r_lnw", [1, DC, 128], F32)
    lnb = C.sb("r_lnb", [1, DC, 128], F32)
    gnw = C.sb("r_gnw", [128, 4], F32)
    cosF = C.sb("r_cos", [128, T], BF16)
    sinF = C.sb("r_sin", [128, T], BF16)
    dmt = C.sb("r_dmt", [128, 4, CH], F32)
    cdec = C.sb("r_cdec", [128, 4, CH], BF16)
    kdec = C.sb("r_kdec", [128, 4], F32)
    Ssb = C.sb("r_S", [128, 8, 512], F32)
    Sb16 = C.sb("r_S16", [128, 8, 512], BF16)
    wqb = C.sb("r_wq", [128, DC, 256], BF16)
    wkb = C.sb("r_wk", [128, DC, 256], BF16)
    wvb = C.sb("r_wv", [128, DC, 512], BF16)
    wgb = C.sb("r_wg", [128, DC, 512], BF16)
    wob = C.sb("r_wo", [128, 4, 1024], BF16)
    qh = C.sb("r_qh", [128, 2, TSB], BF16)
    kh = C.sb("r_kh", [128, 2, TSB], BF16)
    vtok = C.sb("r_vt", [128, NCH, 512], BF16)
    gsil = C.sb("r_gs", [128, 4, TSB], BF16)
    og = C.sb("r_og", [128, 4, TSB], BF16)
    ktil = C.sb("r_kt", [128, 2, CH], BF16)
    qtl = C.sb("r_qtl", [128, 2, CH], BF16)
    Pt = C.sb("r_Pt", [128, CH], BF16)
    osq = C.sb("r_osq", [128, 4, CH], F32)
    ofp = C.sb("r_ofp", [128, 4, CH], BF16)
    rdbl = C.sb("r_rdb", [128, CH], F32)
    ra = C.sb("r_ra", [128, TSB], BF16)
    rb = C.sb("r_rb", [128, TSB], BF16)
    xsl = C.sb("r_xsl", [128, DC, TSB], F32)
    t1 = C.dma("sync", lnw[:], lnw_ap)
    t2 = C.dma("sync", lnb[:], lnb_ap)
    t3 = C.dma("sync", gnw[:], gnw_ap)
    t4 = C.dma("gpsimd", cosF[:], cos_ap)
    t5 = C.dma("gpsimd", sinF[:], sin_ap)
    t6 = C.dma("sync", dmt[:], dmt_ap)
    t7 = C.dma("gpsimd", cdec[:], cdec_ap)
    t8 = C.dma("sync", kdec[:], kdec_ap)
    t9 = C.dma("sync", Ssb[:], sin_state_ap)
    C.wtok("vector", t9)
    C.do("vector", lambda e: e.tensor_copy(Sb16[:], Ssb[:]), inc=("s_s16", 1))
    t9b = ("s_s16", C.cnt("s_s16"))
    prev_store = None
    for sb in range(NSB):
        tsl = slice(sb * TSB, (sb + 1) * TSB)
        C.w("sync", "s_rmm", C.cnt("s_rmm"))
        C.w("sync", "s_aev", C.cnt("s_aev"))
        xtok = C.dma("sync", xbuf[:], x_dram[:, tsl].rearrange("(c p) t -> p c t", p=128),
                     semname="s_xr")
        for e in ("tensor", "vector", "scalar", "gpsimd"):
            C.wtok(e, xtok)
        if sb == 0:
            C.wtok("tensor", t1)
            C.wtok("tensor", t2)
        row = ln_normed(C, xbuf, TSB, double=True, lnw_row=lnw, lnb_row=lnb, out=nx,
                        y_coeffs=True)
        nx_tok = ("s_aev", C.cnt("s_aev"))
        for h in range(4):
            gam = GAMMAS[h]
            gC = gam ** CH
            C.w("gpsimd", "s_rmm", C.cnt("s_rmm"))
            tq = load_weight_direct(C, wq_ap[:, h * 256:(h + 1) * 256], 1024, 256, wqb, semname="s_w0")
            tk = load_weight_direct(C, wk_ap[:, h * 256:(h + 1) * 256], 1024, 256, wkb, semname="s_w0")
            tv = load_weight_direct(C, wv_ap[:, h * 512:(h + 1) * 512], 1024, 512, wvb, semname="s_w0")
            if not state_only:
                tg = load_weight_direct(C, wg_ap[:, h * 512:(h + 1) * 512], 1024, 512, wgb, semname="s_w0")
                tw = load_weight_direct(C, wo_ap[h * 512:(h + 1) * 512, :], 512, 1024, wob, semname="s_w0")
                C.wtok("vector", tw)
                if h == 0 and sb == 0:
                    C.wtok("vector", t3)
                for ic in range(4):
                    C.do("vector", lambda e, ic=ic: e.tensor_scalar(
                        wob[:, ic, :], wob[:, ic, :], gnw[:, ic:ic + 1], None, ALU.mult),
                        inc=("s_w0s", 1) if ic == 3 else None)
                tw = ("s_w0s", C.cnt("s_w0s"))
            for dt_ in range(2):
                fns = [(lambda e, ps, first, last, k=k, dt_=dt_: e.matmul(
                    ps[:, :], wkb[:, k, dt_ * 128:(dt_ + 1) * 128], nx[:, k, :],
                    start=first, stop=last)) for k in range(DC)]
                ps = C.rot_tile(fns, TGT=TSB, mm_dep=max_tok(nx_tok, tk)
                                if dt_ == 0 else None)
                C.rot_evac("vector", [lambda e, ps=ps, dt_=dt_: e.tensor_copy(
                    kh[:, dt_, :], ps[:, 0:TSB])])
            if not state_only:
                for dt_ in range(2):
                    fns = [(lambda e, ps, first, last, k=k, dt_=dt_: e.matmul(
                        ps[:, :], wqb[:, k, dt_ * 128:(dt_ + 1) * 128], nx[:, k, :],
                        start=first, stop=last)) for k in range(DC)]
                    ps = C.rot_tile(fns, TGT=TSB, mm_dep=tq if dt_ == 0 else None)
                    C.rot_evac("vector", [lambda e, ps=ps, dt_=dt_: e.tensor_copy(
                        qh[:, dt_, :], ps[:, 0:TSB])])
            for tt in range(NCH):
                fns = [(lambda e, ps, first, last, k=k, tt=tt: e.matmul(
                    ps[:, :], nx[:, k, tt * 128:(tt + 1) * 128], wvb[:, k, :],
                    start=first, stop=last)) for k in range(DC)]
                ps = C.rot_tile(fns, mm_dep=tv if tt == 0 else None)
                C.rot_evac("vector", [lambda e, ps=ps, tt=tt: e.tensor_copy(
                    vtok[:, tt, :], ps[:, 0:512])])
            if not state_only:
                for dg in range(4):
                    fns = [(lambda e, ps, first, last, k=k, dg=dg: e.matmul(
                        ps[:, :], wgb[:, k, dg * 128:(dg + 1) * 128], nx[:, k, :],
                        start=first, stop=last)) for k in range(DC)]
                    ps = C.rot_tile(fns, TGT=TSB, mm_dep=tg if dg == 0 else None)
                    C.rot_evac("scalar", [lambda e, ps=ps, dg=dg: e.activation(
                        gsil[:, dg, :], ps[:, 0:TSB], AF.Silu)])
            rdep = max_tok(t4, t5) if h == 0 and sb == 0 else None
            projdone = ("s_rv_vector", C.cnt("s_rv_vector"))
            C.w("vector", projdone[0], projdone[1])
            if rdep is not None:
                C.w("vector", rdep[0], rdep[1])
            csl = cosF[:, tsl]
            ssl = sinF[:, tsl]
            bufs = (kh,) if state_only else (kh, qh)
            for buf in bufs:
                C.do("vector", lambda e, buf=buf, csl=csl: e.tensor_mul(ra[:], buf[:, 0, :], csl))
                C.do("vector", lambda e, buf=buf, ssl=ssl: e.tensor_mul(rb[:], buf[:, 0, :], ssl))
                C.do("vector", lambda e, buf=buf, ssl=ssl: e.tensor_mul(buf[:, 0, :], buf[:, 1, :], ssl))
                C.do("vector", lambda e, buf=buf: e.tensor_sub(buf[:, 0, :], ra[:], buf[:, 0, :]))
                C.do("vector", lambda e, buf=buf, csl=csl: e.tensor_mul(buf[:, 1, :], buf[:, 1, :], csl))
                C.do("vector", lambda e, buf=buf: e.tensor_add(buf[:, 1, :], buf[:, 1, :], rb[:]))
            C.do("vector", lambda e: e.engine_nop(), inc=("s_rope", 1))
            C.w("tensor", "s_rope", C.cnt("s_rope"))
            for ch in range(NCH):
                chs = slice(ch * 128, (ch + 1) * 128)
                for dt_ in range(2):
                    ps_ap, tpv = pe_transpose(C, kh[:, dt_, chs])
                    tp_consume(C, "vector", [
                        lambda e, ps_ap=ps_ap, dt_=dt_, h=h: e.tensor_scalar(
                            ktil[:, dt_, :], ps_ap, kdec[:, h:h + 1], None, ALU.mult)],
                        tpv)
                C.do("vector", lambda e: e.engine_nop(), inc=("s_kt", 1))
                kt_tok = ("s_kt", C.cnt("s_kt"))
                if not state_only:
                    C.w("vector", "s_rmm", C.cnt("s_rmm"))
                    for dt_ in range(2):
                        C.do("vector", lambda e, dt_=dt_, h=h, chs=chs: e.tensor_mul(
                            qtl[:, dt_, :], qh[:, dt_, chs], cdec[:, h, :]))
                    C.do("vector", lambda e: e.engine_nop(), inc=("s_qt", 1))
                    qt_tok = ("s_qt", C.cnt("s_qt"))
                    fns = [(lambda e, ps, first, last, dt_=dt_, chs=chs: e.matmul(
                        ps[:, 0:CH], kh[:, dt_, chs], qh[:, dt_, chs],
                        start=first, stop=last)) for dt_ in range(2)]
                    ps = C.rot_tile(fns, TGT=CH)
                    C.rot_evac("vector", [lambda e, ps=ps, h=h: e.tensor_mul(
                        Pt[:], ps[:, 0:CH], dmt[:, h, :])],
                        dep=t6 if h == 0 and sb == 0 and ch == 0 else None)
                    pt_tok = C.rot_ledger[-1]
                    C.w("tensor", pt_tok[0], pt_tok[1])
                    C.w("tensor", qt_tok[0], qt_tok[1])
                    sq_base = C.cnt("s_rv_scalar")
                    for dv in range(4):
                        fns = [(lambda e, ps, first, last, dv=dv, ch=ch: e.matmul(
                            ps[:, 0:CH], vtok[:, ch, dv * 128:(dv + 1) * 128],
                            Pt[:], start=first, stop=False))]
                        fns += [(lambda e, ps, first, last, dv=dv, dt_=dt_, h=h: e.matmul(
                            ps[:, 0:CH], Sb16[:, h * 2 + dt_, dv * 128:(dv + 1) * 128],
                            qtl[:, dt_, :], start=False, stop=(dt_ == 1)))
                            for dt_ in range(2)]
                        ps = C.rot_tile(fns, TGT=CH,
                                        mm_dep=t9b if h == 0 and sb == 0 and ch == 0 and dv == 0 else None)
                        C.rot_evac("scalar", [
                            (lambda e, ps=ps, dv=dv: e.activation(osq[:, dv, :], ps[:, 0:CH],
                                                                  AF.Square)),
                            (lambda e, ps=ps, dv=dv: e.activation(ofp[:, dv, :], ps[:, 0:CH],
                                                                  AF.Copy)),
                        ])
                    C.aux_guard()
                    C.w("tensor", "s_rv_scalar", C.cnt("s_rv_scalar"))
                    for dv in range(4):
                        C.do("tensor", lambda e, dv=dv: e.matmul(
                            C.psU[0:1, 0:CH], C.ones_col[:, 0:1], osq[:, dv, :],
                            start=(dv == 0), stop=(dv == 3)),
                            inc=("s_amm", 1) if dv == 3 else None)
                    C.aux_evac("scalar", [
                        (lambda e: e.activation(rdbl[0:1, :], C.psU[0:1, 0:CH], AF.Copy,
                                                scale=1.0 / 512.0)),
                    ])
                    C.w("vector", "s_aev", C.cnt("s_aev"))
                    C.do("vector", lambda e: e.tensor_scalar(rdbl[0:1, :], rdbl[0:1, :],
                                                             EPS, None, ALU.add),
                         inc=("s_gv", 1))
                    C.w("scalar", "s_gv", C.cnt("s_gv"))
                    C.do("scalar", lambda e: e.activation(rdbl[0:1, :], rdbl[0:1, :], AF.Ln))
                    C.do("scalar", lambda e: e.activation(rdbl[0:1, :], rdbl[0:1, :],
                                                          AF.Exp, scale=-0.5),
                         inc=("s_gs", 1))
                    C.aux_guard()
                    C.w("tensor", "s_gs", C.cnt("s_gs"))
                    C.do("tensor", lambda e: e.matmul(C.psS[:, 0:CH], C.ones128[0:1, :],
                                                      rdbl[0:1, :], start=True, stop=True),
                         inc=("s_amm", 1))
                    C.aux_evac("vector", [lambda e: e.tensor_copy(rdbl[:], C.psS[:, 0:CH])])
                    rd_tok = ("s_aev", C.cnt("s_aev"))
                    C.w("vector", rd_tok[0], rd_tok[1])
                    for dv in range(4):
                        C.do("vector", lambda e, dv=dv, chs=chs: e.tensor_mul(
                            og[:, dv, chs], ofp[:, dv, :], gsil[:, dv, chs]))
                        C.do("vector", lambda e, dv=dv, chs=chs: e.tensor_mul(
                            og[:, dv, chs], og[:, dv, chs], rdbl[:]))
                    C.do("vector", lambda e: e.engine_nop(), inc=("s_og", 1))
                C.w("tensor", kt_tok[0], kt_tok[1])
                for dt_ in range(2):
                    fns = [(lambda e, ps, first, last, dt_=dt_, ch=ch: e.matmul(
                        ps[:, 0:512], ktil[:, dt_, :], vtok[:, ch, :],
                        start=first, stop=last))]
                    ps = C.rot_tile(fns, TGT=512)
                    C.rot_evac("vector", [
                        (lambda e, dt_=dt_, h=h, gC=gC: e.tensor_scalar(
                            Ssb[:, h * 2 + dt_, :], Ssb[:, h * 2 + dt_, :],
                            gC, None, ALU.mult)),
                        (lambda e, ps=ps, dt_=dt_, h=h: e.scalar_tensor_tensor(
                            Ssb[:, h * 2 + dt_, :], ps[:, 0:512], 1.0,
                            Ssb[:, h * 2 + dt_, :], ALU.mult, ALU.add)),
                        (lambda e, dt_=dt_, h=h: e.tensor_copy(
                            Sb16[:, h * 2 + dt_, :], Ssb[:, h * 2 + dt_, :])),
                    ], dep=t9 if h == 0 and sb == 0 and ch == 0 and dt_ == 0 else None)
                supd_tok = C.rot_ledger[-1]
                C.w("tensor", supd_tok[0], supd_tok[1])
            if not state_only:
                og_tok = ("s_og", C.cnt("s_og"))
                if h == 0:
                    txl = C.dma("sync", xsl[:], x_dram[:, tsl]
                                .rearrange("(c p) t -> p c t", p=128), semname="s_x2r")
                for do in range(DC):
                    fns = [(lambda e, ps, first, last, k=k, do=do: e.matmul(
                        ps[:, 0:TSB], wob[:, k, do * 128:(do + 1) * 128],
                        og[:, k, :], start=first, stop=last)) for k in range(4)]
                    ps = C.rot_tile(fns, TGT=TSB, mm_dep=max_tok(tw, og_tok)
                                    if do == 0 else None)
                    deps = []
                    if h == 0 and do == 0:
                        deps = [txl, t3]
                        if prev_store is not None:
                            deps.append(prev_store)
                    C.rot_evac("vector", [lambda e, ps=ps, do=do: e.scalar_tensor_tensor(
                        xsl[:, do, :], ps[:, 0:TSB], 1.0, xsl[:, do, :],
                        ALU.mult, ALU.add)], dep=max_tok(*deps) if deps else None)
        if not state_only:
            C.do("vector", lambda e: e.scalar_tensor_tensor(
                row[0:1, 0, 0:TSB], row[0:1, 1, 0:TSB], -1.0, row[0:1, 4, 0:TSB],
                ALU.mult, ALU.mult), inc=("s_row", 1))
            C.w("tensor", "s_row", C.cnt("s_row"))
            for c in range(DC):
                C.aux_guard()
                C.do("tensor", lambda e, c=c: e.matmul(
                    C.psS[:, 0:TSB], C.ones128[0:1, :], row[0:1, 4, 0:TSB],
                    start=True, stop=True))
                C.do("tensor", lambda e, c=c: e.matmul(
                    C.psO[:, 0:TSB], C.ones128[0:1, :], row[0:1, 0, 0:TSB],
                    start=True, stop=True), inc=("s_amm", 1))
                C.aux_evac("vector", [
                    (lambda e, c=c: e.tensor_mul(C.t_ln[:, 0:TSB], xbuf[:, c, :],
                                                 C.psS[:, 0:TSB])),
                    (lambda e, c=c: e.scalar_tensor_tensor(
                        C.t_ln[:, 0:TSB], C.psO[:, 0:TSB], 1.0,
                        C.t_ln[:, 0:TSB], ALU.mult, ALU.add)),
                    (lambda e, c=c: e.tensor_add(xsl[:, c, :], xsl[:, c, :],
                                                 C.t_ln[:, 0:TSB])),
                ])
            C.do("vector", lambda e: e.engine_nop(), inc=("s_xar", 1))
            C.w("sync", "s_xar", C.cnt("s_xar"))
            prev_store = C.dma("sync", xout_dram[:, tsl]
                               .rearrange("(c p) t -> p c t", p=128), xsl[:],
                               semname="s_xsr")
    if prev_store is not None:
        C.wtok("sync", prev_store)
    full_barrier_on(C, "sync")
    if sout_ap is not None:
        ts_ = C.dma("sync", sout_ap, Ssb[:])
        C.wtok("sync", ts_)


def NSB_stores(C):
    return C.ct["s_xsr"].n // 16 if "s_xsr" in C.ct else 0



def _cols(b):
    return np.ascontiguousarray(np.asarray(b, np.float32).reshape(8, 128).T)


def _rw3(b):
    return np.ascontiguousarray(np.asarray(b, np.float32).reshape(1, 8, 128))


def _l0_consts(core):
    half = core % 2
    dk = 256
    gam = np.array(GAMMAS)
    inv = 1.0 / (10000.0 ** (np.arange(0, dk, 2) / dk))
    pos = np.arange(half * 2048, (half + 1) * 2048)
    ang = np.outer(inv, pos)
    i = np.arange(128)
    dmt = np.zeros((128, 4, 128), np.float32)
    cdec = np.zeros((128, 4, 128), np.float32)
    for h in range(4):
        dmt[:, h, :] = np.where(i[None, :] >= i[:, None],
                                gam[h] ** (i[None, :] - i[:, None]), 0.0) / 16.0
        cdec[:, h, :] = (gam[h] ** (i + 1))[None, :] / 16.0
    kdec = np.stack([gam[h] ** (127 - i) for h in range(4)], -1).astype(np.float32)
    return {"r_cosc": np.cos(ang).astype(BF), "r_sinc": np.sin(ang).astype(BF),
            "r_dmtc": dmt, "r_cdecc": cdec.astype(BF), "r_kdecc": kdec}


def _l3_rope(core):
    half = core % 2
    HD = 64
    inv = 1.0 / (10000.0 ** (np.arange(0, HD, 2) / HD))
    t = np.arange(half * 2048, (half + 1) * 2048, dtype=np.float64)
    fr = np.outer(t, inv)
    emb = np.concatenate([fr, fr], -1)
    c = np.cos(emb).T
    s = np.sin(emb).T
    return (np.concatenate([c, c], 0).astype(BF),
            np.concatenate([s, s], 0).astype(BF))


DBG_FIN = False


def _ckfin(tag, arrs):
    if DBG_FIN:
        np.save('/root/problem/scratch/stage_%s.npy' % tag,
                np.stack([np.asarray(a) for a in arrs]))
        import sys
        bad = [i for i, a in enumerate(arrs) if not np.isfinite(np.asarray(a, np.float32)).all()]
        mx = max(np.abs(np.asarray(a, np.float32)).max() for a in arrs)
        print(f"[fin] {tag}: bad_cores={bad} max={mx:.3e}", file=sys.stderr)


def kernel(**inp):
    B, T, Dm = 4, 4096, 1024
    x = np.ascontiguousarray(np.asarray(inp["L0_attn_x"], np.float32))
    xs = [np.ascontiguousarray(x[c // 2, (c % 2) * 2048:(c % 2 + 1) * 2048, :].T)
          for c in range(8)]

    def build_l0(state_only):
        C = Launch("l0s" if state_only else "l0f")
        C.setup_common(maxT=512, onesT=True)
        xd = C.inp("x_in", [1024, 2048])
        xo = None if state_only else C.out("x_out", [1024, 2048])
        so = C.out("s_out", [128, 8, 512]) if state_only else None
        emit_l0attn(C, xd, xo, C.inp("wq", [1024, 1024]), C.inp("wk", [1024, 1024]),
                    C.inp("wv", [1024, 2048]), C.inp("wg", [1024, 2048]),
                    C.inp("wo", [2048, 1024]),
                    C.inp("lnwr", [1, 8, 128]), C.inp("lnbr", [1, 8, 128]),
                    C.inp("gnwc", [128, 4]),
                    C.inp("r_cosc", [128, 2048], BF16), C.inp("r_sinc", [128, 2048], BF16),
                    C.inp("r_dmtc", [128, 4, 128]), C.inp("r_cdecc", [128, 4, 128], BF16),
                    C.inp("r_kdecc", [128, 4]), C.inp("s_in", [128, 8, 512]),
                    sout_ap=so, state_only=state_only)
        return C
    l0base = {"wq": inp["L0_attn_Wq"], "wk": inp["L0_attn_Wk"], "wv": inp["L0_attn_Wv"],
              "wg": inp["L0_attn_Wg"], "wo": inp["L0_attn_Wo"],
              "lnwr": _rw3(inp["L0_attn_ln_w"]), "lnbr": _rw3(inp["L0_attn_ln_b"]),
              "gnwc": np.ascontiguousarray(
                  np.asarray(inp["L0_attn_g_norm_w"], np.float32).reshape(4, 128).T)}
    z0 = np.zeros((128, 8, 512), np.float32)
    C = build_l0(True)
    maps = []
    for c in range(8):
        m = dict(l0base); m.update(_l0_consts(c)); m["x_in"] = xs[c]; m["s_in"] = z0
        maps.append(m)
    res = C.run(maps)
    Ss = [res.results[c]["s_out"] for c in range(8)]
    _ckfin("l0s", Ss)
    C = build_l0(False)
    maps = []
    for c in range(8):
        m = dict(l0base); m.update(_l0_consts(c)); m["x_in"] = xs[c]
        m["s_in"] = Ss[c - 1] if c % 2 == 1 else z0
        maps.append(m)
    res = C.run(maps)
    xs = [res.results[c]["x_out"] for c in range(8)]
    _ckfin("l0f", xs)

    C = build_l0ffn_launch(2048)
    maps = [{"x_in": xs[c], "w_gate": inp["L0_ffn_w_gate"],
             "w_up": inp["L0_ffn_w_up"], "w_down": inp["L0_ffn_w_down"]}
            for c in range(8)]
    res = C.run(maps)
    xs = [res.results[c]["x_out"] for c in range(8)]
    _ckfin("l0ffn", xs)

    def build_l1(with_outs):
        C = Launch("l1a")
        C.setup_common(maxT=512, onesT=True)
        xd = C.inp("x_in", [1024, 2048])
        xo = C.out("x_out", [1024, 2048])
        co = C.out("carry_out", [128, 8, 2]) if with_outs else None
        nlo = C.out("nxlast", [128, 8, 1], BF16) if with_outs else None
        emit_l1attn(C, xd, xo, C.inp("wk", [1024, 1024]), C.inp("wv", [1024, 1024]),
                    C.inp("wr", [1024, 1024]), C.inp("wo", [1024, 1024]),
                    C.inp("lngr", [1, 8, 128]), C.inp("lnbr", [1, 8, 128]),
                    C.inp("tmkc", [128, 8]), C.inp("tmvc", [128, 8]),
                    C.inp("tmrc", [128, 8]), C.inp("tdl", [128, 8]),
                    C.inp("nxpc", [128, 8, 1], BF16), C.inp("carryc", [128, 8, 2]),
                    carryout_ap=co, nxlast_ap=nlo)
        return C
    l1base = {"wk": inp["L1_attn_Wk"], "wv": inp["L1_attn_Wv"], "wr": inp["L1_attn_Wr"],
              "wo": inp["L1_attn_Wo"], "lngr": _rw3(inp["L1_attn_ln_g"]),
              "lnbr": _rw3(inp["L1_attn_ln_b"]),
              "tmkc": _cols(np.asarray(inp["L1_attn_time_mix_k"]).reshape(-1)),
              "tmvc": _cols(np.asarray(inp["L1_attn_time_mix_v"]).reshape(-1)),
              "tmrc": _cols(np.asarray(inp["L1_attn_time_mix_r"]).reshape(-1)),
              "tdl": _cols(inp["L1_attn_time_decay"])}
    zc = np.zeros((128, 8, 2), np.float32)
    zn = np.zeros((128, 8, 1), BF)
    C = build_l1(True)
    maps = []
    for c in range(8):
        m = dict(l1base); m["x_in"] = xs[c]; m["nxpc"] = zn; m["carryc"] = zc
        maps.append(m)
    res = C.run(maps)
    carr = [res.results[c]["carry_out"] for c in range(8)]
    nxl = [res.results[c]["nxlast"] for c in range(8)]
    x1p = [res.results[c]["x_out"] for c in range(8)]
    C = build_l1(False)
    maps = []
    for c in range(8):
        m = dict(l1base); m["x_in"] = xs[c]
        m["nxpc"] = nxl[c - 1] if c % 2 == 1 else zn
        m["carryc"] = carr[c - 1] if c % 2 == 1 else zc
        maps.append(m)
    res = C.run(maps)
    xs = [x1p[c] if c % 2 == 0 else res.results[c]["x_out"] for c in range(8)]
    _ckfin("l1a", xs)

    C = Launch("l1f")
    C.setup_common(maxT=2048)
    xd = C.inp("x_in", [1024, 2048])
    xo = C.out("x_out", [1024, 2048])
    emit_l1ffn(C, xd, xo, 2048,
               wg_ap=C.inp("wg", [1024, 8]), bgbc_ap=C.inp("bgbc", [128, 128]),
               w1_ap=C.inp("w1", [8, 1024, 4096]), b1t_ap=C.inp("b1t", [4096, 8]),
               w2_ap=C.inp("w2", [8, 4096, 1024]), b2bc_ap=C.inp("b2bc", [128, 8, 1024]),
               gbc_ap=C.inp("gbc", [128, 1024]), bbc_ap=C.inp("bbc", [128, 1024]))
    l1f = {"wg": inp["L1_ffn_Wg"],
           "bgbc": np.ascontiguousarray(np.broadcast_to(
               np.tile(np.asarray(inp["L1_ffn_bg"], np.float32), 16), (128, 128))),
           "w1": inp["L1_ffn_W1"],
           "b1t": np.ascontiguousarray(np.asarray(inp["L1_ffn_b1"], np.float32).T),
           "w2": inp["L1_ffn_W2"],
           "b2bc": np.ascontiguousarray(np.broadcast_to(
               np.asarray(inp["L1_ffn_b2"], np.float32)[None], (128, 8, 1024))),
           "gbc": np.ascontiguousarray(np.broadcast_to(
               np.asarray(inp["L1_ffn_gamma"], np.float32), (128, 1024))),
           "bbc": np.ascontiguousarray(np.broadcast_to(
               np.asarray(inp["L1_ffn_beta"], np.float32), (128, 1024)))}
    maps = [dict(l1f, x_in=xs[c]) for c in range(8)]
    res = C.run(maps)
    xs = [res.results[c]["x_out"] for c in range(8)]
    _ckfin("l1ffn", xs)

    C = Launch("l2a")
    C.setup_common(maxT=1024, onesT=False)
    xo = C.out("x_out", [1024, 2048])
    emit_l2attn(C, C.inp("x_in", [1024, 2560]), xo,
                C.inp("wqkv", [3072, 1024]), C.inp("bqkvc", [128, 8, 3]),
                C.inp("wproj", [1024, 1024]), C.inp("bpc", [128, 8]),
                C.inp("maskc", [128, 4, 2, 4]), C.inp("a2_bvbc", [128, 1024]))
    bqkv = np.asarray(inp["L2_attn_b_qkv"], np.float32)
    l2b = {"wqkv": inp["L2_attn_w_qkv"],
           "bqkvc": np.stack([_cols(bqkv[0:1024]), _cols(bqkv[1024:2048]),
                              _cols(bqkv[2048:3072])], axis=-1),
           "wproj": inp["L2_attn_w_proj"], "bpc": _cols(inp["L2_attn_b_proj"]),
           "a2_bvbc": np.ascontiguousarray(np.broadcast_to(bqkv[2048:3072], (128, 1024)))}
    maps = []
    for c in range(8):
        b, half = c // 2, c % 2
        ext = np.zeros((1024, 2560), np.float32)
        gs = half * 2048 - 256
        lo, hi = max(gs, 0), min(half * 2048 + 2048 + 256, 4096)
        for cc in (2 * b, 2 * b + 1):
            h2 = cc % 2
            s0, s1 = h2 * 2048, h2 * 2048 + 2048
            a, bnd = max(lo, s0), min(hi, s1)
            if a < bnd:
                ext[:, a - gs:bnd - gs] = xs[cc][:, a - s0:bnd - s0]
        mask = np.zeros((128, 4, 2, 4), np.float32)
        if half == 0:
            mask[:, 0, 0, 0] = -100.0
        else:
            mask[:, 3, 1, 3] = -100.0
        maps.append(dict(l2b, x_in=ext, maskc=mask))
    res = C.run(maps)
    xs = [res.results[c]["x_out"] for c in range(8)]
    _ckfin("l2a", xs)

    C = Launch("l2f")
    C.setup_common(maxT=2048)
    x_sb = stream_in(C, 2048)
    aps = l2ffn_declare(C)
    emit_l2ffn(C, x_sb, 2048, **aps)
    stream_out(C, x_sb, 2048)
    f2 = l2ffn_host_inputs(inp)
    maps = [dict(f2, x_in=xs[c]) for c in range(8)]
    res = C.run(maps)
    xs = [res.results[c]["x_out"] for c in range(8)]
    _ckfin("l2ffn", xs)

    C = Launch("l3a")
    C.setup_common(maxT=1024)
    qo = C.out("qo", [128, 8, 2048], BF16)
    ko = C.out("ko", [128, 8, 2048], BF16)
    vo = C.out("vo", [128, 16, 1024], BF16)
    emit_l3attn_qkv(C, C.inp("x_in", [1024, 2048]), 2048,
                    C.inp("wq", [1024, 1024]), C.inp("wk", [1024, 1024]),
                    C.inp("wv", [1024, 1024]),
                    C.inp("bqc", [128, 8]), C.inp("bkc", [128, 8]),
                    C.inp("bvb", [128, 1024]),
                    C.inp("cosF", [128, 2048], BF16), C.inp("sinF", [128, 2048], BF16),
                    qo, ko, vo)
    l3b_ = {"wq": inp["L3_attn_wq"], "wk": inp["L3_attn_wk"], "wv": inp["L3_attn_wv"],
            "bqc": _cols(inp["L3_attn_bq"]), "bkc": _cols(inp["L3_attn_bk"]),
            "bvb": np.ascontiguousarray(np.broadcast_to(
                np.asarray(inp["L3_attn_bv"], np.float32), (128, 1024)))}
    maps = []
    for c in range(8):
        ct, st = _l3_rope(c)
        maps.append(dict(l3b_, x_in=xs[c], cosF=ct, sinF=st))
    res = C.run(maps)
    qs_ = [res.results[c]["qo"] for c in range(8)]
    ks_ = [res.results[c]["ko"] for c in range(8)]
    vs_ = [res.results[c]["vo"] for c in range(8)]

    C = Launch("l3b")
    C.setup_common(maxT=512)
    xd = C.inp("x_in", [1024, 2048])
    xo = C.out("x_out", [1024, 2048])
    emit_l3attn_core(C, xd, xo, 2048, 4096,
                     C.inp("qf", [128, 8, 2048], BF16), C.inp("kf", [128, 8, 4096], BF16),
                     C.inp("vf", [128, 32, 1024], BF16),
                     C.inp("wo", [1024, 1024]), C.inp("boc", [128, 8]))
    maps = []
    for c in range(8):
        b = c // 2
        kf = np.concatenate([ks_[2 * b], ks_[2 * b + 1]], axis=2)
        vf = np.concatenate([vs_[2 * b], vs_[2 * b + 1]], axis=1)
        maps.append({"x_in": xs[c], "qf": qs_[c], "kf": kf, "vf": vf,
                     "wo": inp["L3_attn_wo"], "boc": _cols(inp["L3_attn_bo"])})
    res = C.run(maps)
    xs = [res.results[c]["x_out"] for c in range(8)]
    _ckfin("l3b", xs)

    C = Launch("l3f")
    C.setup_common(maxT=2048)
    xd = C.inp("x_in", [1024, 2048])
    xo = C.out("x_out", [1024, 2048])
    emit_l3ffn(C, xd, xo, 2048, C.inp("wr", [1024, 8]), C.inp("wg", [8, 1024, 1024]),
               C.inp("wu", [8, 1024, 1024]), C.inp("wd", [8, 1024, 1024]),
               C.inp("temp", [1, 1]))
    l3f = {"wr": inp["L3_ffn_Wr"], "wg": inp["L3_ffn_Wg"], "wu": inp["L3_ffn_Wu"],
           "wd": inp["L3_ffn_Wd"],
           "temp": np.asarray(inp["L3_ffn_temperature"], np.float32).reshape(1, 1)}
    maps = [dict(l3f, x_in=xs[c]) for c in range(8)]
    res = C.run(maps)
    xs = [res.results[c]["x_out"] for c in range(8)]

    _ckfin("l3f", xs)
    out = np.empty((4, 4096, 1024), np.float32)
    for c in range(8):
        out[c // 2, (c % 2) * 2048:(c % 2 + 1) * 2048, :] = np.asarray(xs[c]).T
    return out
